# version 23
# baseline (speedup 1.0000x reference)
import numpy as np
import ml_dtypes
import concourse.bass as bass
import concourse.mybir as mybir
from concourse.bass_utils import run_bass_kernel_spmd

F32 = mybir.dt.float32
BF16 = mybir.dt.bfloat16
I32 = mybir.dt.int32
AF = mybir.ActivationFunctionType
ALU = mybir.AluOpType
AX = mybir.AxisListType

B, S, D = 4, 4096, 1024
H, DH = 16, 64
SC = 2048
NT = 17
MAGIC = 12582912.0
EPS = 1e-5


class Buf:
    __slots__ = ("w", "r")

    def __init__(self):
        self.w = None
        self.r = []


class Sched:
    ENGINES = ("tensor", "vector", "scalar", "sync", "gpsimd")
    DMA_SEMS = ("lds", "ldw", "ldx", "wt", "out", "tr1", "tr2", "tr3")

    def __init__(self):
        self.ops = []
        self.last_of_dma = {d: None for d in self.DMA_SEMS}
        self.own = {"tensor": "pe", "vector": "dve", "scalar": "act",
                    "sync": "spq", "gpsimd": "gpq"}

    def do(self, engine, emit, rd=(), wr=(), dma_sem=None, keep_pe=False, also=()):
        oid = len(self.ops)
        deps = set(also)
        if dma_sem is not None and self.last_of_dma[dma_sem] is not None:
            deps.add(self.last_of_dma[dma_sem])
        for b in rd:
            if b.w is not None:
                deps.add(b.w)
        for b in wr:
            if b.w is not None:
                deps.add(b.w)
            deps.update(b.r)
        self.ops.append((engine, emit, deps, dma_sem, keep_pe))
        if dma_sem is not None:
            self.last_of_dma[dma_sem] = oid
        for b in rd:
            b.r.append(oid)
        for b in wr:
            b.w = oid
            b.r = []
        return oid

    def emit(self, block, sems, final_waits=()):
        n = len(self.ops)
        engine_of = [o[0] for o in self.ops]
        dma_of = [o[3] for o in self.ops]
        keep_of = [o[4] for o in self.ops]
        sem_of = [dma_of[i] if dma_of[i] else self.own[engine_of[i]]
                  for i in range(n)]
        awaited = [False] * n
        for i, (eng, _e, deps, _d, _k) in enumerate(self.ops):
            for d in deps:
                if engine_of[i] == "tensor" and engine_of[d] == "tensor" \
                        and dma_of[d] is None and not keep_of[i]:
                    continue
                awaited[d] = True
        for d in final_waits:
            awaited[d] = True
        val = [None] * n
        cum = {}
        for i in range(n):
            if dma_of[i] is not None:
                awaited[i] = True
            if awaited[i]:
                sm = sem_of[i]
                cum[sm] = cum.get(sm, 0) + (16 if dma_of[i] else 1)
                val[i] = cum[sm]
        qs = {e: [] for e in self.ENGINES}
        last_wait = {e: {} for e in self.ENGINES}
        for i, (eng, emitfn, deps, dma, _k) in enumerate(self.ops):
            need = {}
            for d in deps:
                if engine_of[i] == "tensor" and engine_of[d] == "tensor" \
                        and dma_of[d] is None and not keep_of[i]:
                    continue
                sm, v = sem_of[d], val[d]
                assert v is not None
                if need.get(sm, 0) < v:
                    need[sm] = v
            waits = []
            lw = last_wait[eng]
            for sm, v in need.items():
                if lw.get(sm, 0) < v:
                    waits.append((sm, v))
                    lw[sm] = v
            inc = (sem_of[i], 16 if dma else 1) if awaited[i] else None
            qs[eng].append((waits, emitfn, inc))

        def mk(ops):
            def _(eng):
                for waits, emitfn, inc in ops:
                    for sm, v in waits:
                        eng.wait_ge(sems[sm], v)
                    instr = emitfn(eng)
                    if inc is not None:
                        instr.then_inc(sems[inc[0]], inc[1])
            return _
        for e in self.ENGINES:
            if qs[e]:
                getattr(block, e)(mk(qs[e]))
        return {d: (sem_of[d], val[d]) for d in final_waits}


def build_nc():
    nc = bass.Bass("TRN2", target_bir_lowering=False)

    xin = nc.dram_tensor("xin", [NT * 128, D], F32, kind="ExternalInput").ap()
    wq = nc.dram_tensor("wq", [D, D], F32, kind="ExternalInput").ap()
    wk = nc.dram_tensor("wk", [D, D], F32, kind="ExternalInput").ap()
    wv = nc.dram_tensor("wv", [D, D], F32, kind="ExternalInput").ap()
    wo = nc.dram_tensor("wo", [D, D], F32, kind="ExternalInput").ap()
    tern = nc.dram_tensor("tern", [D, D], I32, kind="ExternalInput").ap()
    bqh = nc.dram_tensor("bqh", [128, D], BF16, kind="ExternalInput").ap()
    bkh = nc.dram_tensor("bkh", [128, D], BF16, kind="ExternalInput").ap()
    bvh = nc.dram_tensor("bvh", [128, D], BF16, kind="ExternalInput").ap()
    boh = nc.dram_tensor("boh", [128, D], BF16, kind="ExternalInput").ap()
    bfh = nc.dram_tensor("bfh", [128, D], BF16, kind="ExternalInput").ap()
    ash = nc.dram_tensor("ash", [128, 1], F32, kind="ExternalInput").ap()
    ckh = nc.dram_tensor("ckh", [NT * 128, DH], BF16, kind="ExternalInput").ap()
    skh = nc.dram_tensor("skh", [NT * 128, DH], BF16, kind="ExternalInput").ap()
    msh = nc.dram_tensor("msh", [128, 256], BF16, kind="ExternalInput").ap()
    mj0h = nc.dram_tensor("mj0h", [128, 128], BF16, kind="ExternalInput").ap()
    mA1h = nc.dram_tensor("mA1h", [128, 256], BF16, kind="ExternalInput").ap()
    idfh = nc.dram_tensor("idfh", [128, 128], F32, kind="ExternalInput").ap()
    idbh = nc.dram_tensor("idbh", [128, 128], BF16, kind="ExternalInput").ap()
    eohh = nc.dram_tensor("eohh", [16, 16 * 128], BF16, kind="ExternalInput").ap()
    xout = nc.dram_tensor("xout", [SC, D], F32, kind="ExternalOutput").ap()

    s = Sched()
    ctx = []

    def sb(name, shape, dt):
        t = nc.sbuf_tensor(name, shape, dt)
        ctx.append(t)
        return t.__enter__()

    def ps(name, shape, dt):
        t = nc.psum_tensor(name, shape, dt)
        ctx.append(t)
        return t.__enter__()

    XR = 6
    xs = [sb(f"xs{i}", [128, D], F32) for i in range(XR)]
    xnst = [sb("xnst0", [128, D], BF16)] * 2
    xnT = [sb(f"xnT{i}", [128, D], BF16) for i in range(2)]
    qb = sb("qb", [128, D], BF16)
    t2 = sb("t2", [128, D], BF16)
    q_tm = [sb("qtm0", [128, D], BF16)] * 2
    k_tm = [sb("ktm0", [128, D], BF16)] * 2
    qTd = [sb(f"qTd{i}", [128, 2 * D], BF16) for i in range(3)]
    kTd = [sb(f"kTd{i}", [128, 2 * D], BF16) for i in range(3)]
    vaug = [sb(f"vaug{i}", [128, H * 65], BF16) for i in range(4)]
    expT = [sb(f"expT{i}", [128, H * 256], BF16) for i in range(2)]
    attn = [sb("attn0", [128, D], BF16)] * 2
    attT = [sb("attT0", [128, D], BF16)] * 2
    xn2 = sb("xn2", [128, D], F32)
    xint = [sb("xint0", [128, D], BF16)] * 2
    xiT = [sb("xiT0", [128, D], BF16)] * 2
    wqT = sb("wqT", [128, 8 * D], BF16)
    wkT = sb("wkT", [128, 8 * D], BF16)
    wvT = sb("wvT", [128, 8 * D], BF16)
    woT = sb("woT", [128, 8 * D], BF16)
    wfT = sb("wfT", [128, 8 * D], BF16)
    bq = sb("bq", [128, D], BF16)
    bk = sb("bk", [128, D], BF16)
    bv = sb("bv", [128, D], BF16)
    bo = sb("bo", [128, D], BF16)
    bf_ = sb("bf_", [128, D], BF16)
    as128 = sb("as128", [128, 1], F32)
    smax = sb("smax", [128, 1], F32)
    rs128 = sb("rs128", [128, 1], F32)
    ck = sb("ck", [128, NT * DH], BF16)
    sk_ = sb("sk_", [128, NT * DH], BF16)
    mstd = sb("mstd", [128, 256], BF16)
    mj0 = sb("mj0", [128, 128], BF16)
    mA1 = sb("mA1", [128, 256], BF16)
    idf = sb("idf", [128, 128], F32)
    idb = sb("idb", [128, 128], BF16)
    eoh = sb("eoh", [16, 16 * 128], BF16)
    nmT = sb("nmT", [16, NT * 128], BF16)
    nmt = sb("nmt", [128, 16], F32)
    bna = sb("bna", [128, 2], F32)
    bns = sb("bns", [128, 12], F32)
    lnt = sb("lnt", [128, 4], F32)
    bna2 = sb("bna2", [128, 2], F32)
    bns2 = sb("bns2", [128, 12], F32)
    lnt2 = sb("lnt2", [128, 4], F32)
    den = sb("den", [128, 16], F32)
    rec = sb("rec", [128, 16], F32)
    cst = sb("cst", [128, 2], F32)
    wstg = sb("wstg", [128, 2 * D], BF16)
    nc.const_aps.register(F32, EPS, cst[:, 0:1])
    nc.const_aps.register(F32, -0.5, cst[:, 1:2])

    bank = [ps(f"pb{i}", [128, 512], F32) for i in range(8)]

    bXS = [Buf() for _ in range(XR)]
    _bxn = Buf(); bXNS = [_bxn, _bxn]
    bXNT = [Buf() for _ in range(2)]
    bQB, bT2 = Buf(), Buf()
    _bq = Buf(); _bk = Buf()
    bQTM = [_bq, _bq]
    bKTM = [_bk, _bk]
    bQTdL = [Buf() for _ in range(3)]
    bQTdR = [Buf() for _ in range(3)]
    bKTdL = [Buf() for _ in range(3)]
    bKTdR = [Buf() for _ in range(3)]
    bVA = [Buf() for _ in range(4)]
    bEXPA = [Buf() for _ in range(2)]
    bEXPB = [Buf() for _ in range(2)]
    _ba = Buf(); bATTN = [_ba, _ba]
    _bat = Buf(); bATTT = [_bat, _bat]
    bXN2 = Buf()
    _bxi = Buf(); bXINT = [_bxi, _bxi]
    _bxt2 = Buf(); bXIT = [_bxt2, _bxt2]
    bW = {n: [Buf() for _ in range(4)] for n in ("wq", "wk", "wv", "wo", "wf")}
    bWSTG = Buf()
    bSMALL = Buf()
    bNMT = [Buf() for _ in range(NT)]
    bNM = Buf()
    bLN = Buf()
    bLN2 = Buf()
    bDEN = Buf()
    bCONST = Buf()
    bBANK = [Buf() for _ in range(8)]
    bSC = Buf()

    wT = {"wq": wqT, "wk": wkT, "wv": wvT, "wo": woT, "wf": wfT}

    trn = [0]

    def dma_T(dst_ap, src_ap, rd, wr):
        sem = ("tr1", "tr1", "tr1")[trn[0] % 3]
        eng = "scalar" if sem == "tr3" else "sync"
        trn[0] += 1
        s.do(eng, lambda e, d=dst_ap, sr=src_ap: e.dma_start_transpose(d, sr),
             rd=rd, wr=wr, dma_sem=sem)


    s.do("vector", lambda e: e.memset(cst[:, 0:1], EPS), wr=[bCONST])
    s.do("vector", lambda e: e.memset(cst[:, 1:2], -0.5), wr=[bCONST])
    s.do("vector", lambda e: e.memset(nmT[:], 0.0), wr=[bNMT[0]])
    for i in range(4):
        s.do("vector", lambda e, i=i: e.memset(
            vaug[i][:].rearrange("p (h c) -> p h c", h=H)[:, :, 64:65], 1.0),
            wr=[bVA[i]])

    def prep_quarter(name, src, hf):
        dst = wT[name]
        for g in range(2):
            s.do("scalar", lambda e, hf=hf, g=g, src=src: e.dma_start(
                out=xs[4 + g][:],
                in_=src[hf * 256 + g * 128: hf * 256 + (g + 1) * 128, :]),
                wr=[bXS[4 + g]], dma_sem="wt")
        for g in range(2):
            if name == "wq":
                s.do("vector", lambda e, g=g: e.tensor_scalar_mul(
                    wstg[:, g * 1024:(g + 1) * 1024], xs[4 + g][:], 0.125),
                    rd=[bXS[4 + g]], wr=[bWSTG])
            else:
                s.do("vector", lambda e, g=g: e.tensor_copy(
                    wstg[:, g * 1024:(g + 1) * 1024], xs[4 + g][:]),
                    rd=[bXS[4 + g]], wr=[bWSTG])
        for og4 in range(2):
            og = hf * 2 + og4
            s.do("sync", lambda e, og=og, og4=og4, dst=dst: e.dma_start_transpose(
                dst[:].rearrange("p (g f) -> p g f", g=8)[:, :, og * 128:(og + 1) * 128],
                wstg[:, og4 * 1024:(og4 + 1) * 1024].rearrange("p (g f) -> p g f", g=8)),
                rd=[bWSTG], wr=[bW[name][hf]], dma_sem="tr1")

    s.do("sync", lambda e: e.dma_start(out=xs[0][:], in_=xin[0:128, :]),
         wr=[bXS[0]], dma_sem="ldx")
    s.do("sync", lambda e: e.dma_start(out=xs[1][:], in_=xin[128:256, :]),
         wr=[bXS[1]], dma_sem="ldx")
    def ln_stats(xbuf, bx, bnsX, bnaX, lntX, bL):
        s.do("vector", lambda e: e.bn_stats(bnsX[:, 0:6], xbuf[:, 0:512]), rd=[bx], wr=[bL])
        s.do("vector", lambda e: e.bn_stats(bnsX[:, 6:12], xbuf[:, 512:1024]), rd=[bx], wr=[bL])
        s.do("vector", lambda e: e.bn_aggr(bnaX[:], bnsX[:]), rd=[bL], wr=[bL])
        s.do("scalar", lambda e: e.activation(lntX[:, 2:3], bnaX[:, 1:2], AF.Ln, bias=EPS),
             rd=[bL, bCONST], wr=[bL])
        s.do("scalar", lambda e: e.activation(lntX[:, 0:1], lntX[:, 2:3], AF.Exp, scale=-0.5),
             rd=[bL, bCONST], wr=[bL])
        s.do("vector", lambda e: e.tensor_scalar(lntX[:, 1:2], bnaX[:, 0:1], lntX[:, 0:1],
                                                 -1.0, op0=ALU.mult, op1=ALU.mult),
             rd=[bL], wr=[bL])

    def ln_apply(xbuf, bx, out, bout, lntX, bL):
        s.do("scalar", lambda e: e.activation(out, xbuf[:], AF.Identity,
                                              bias=lntX[:, 1:2], scale=lntX[:, 0:1]),
             rd=[bx, bL], wr=[bout])

    def rope(proj_banks, bias_sb, j, outbuf, bout):
        pb0, pb1 = proj_banks
        s.do("vector", lambda e: e.scalar_tensor_tensor(
            qb[:, 0:512], bank[pb0][:], 1.0, bias_sb[:, 0:512], op0=ALU.mult, op1=ALU.add),
            rd=[bBANK[pb0], bSMALL], wr=[bQB])
        s.do("vector", lambda e: e.scalar_tensor_tensor(
            qb[:, 512:1024], bank[pb1][:], 1.0, bias_sb[:, 512:1024], op0=ALU.mult, op1=ALU.add),
            rd=[bBANK[pb1], bSMALL], wr=[bQB])
        qb3 = qb[:].rearrange("p (h t f) -> p h t f", h=H, t=2)
        c3 = ck[:, j * 64:(j + 1) * 64].rearrange("p (t f) -> p t f", t=2) \
            .unsqueeze(1).broadcast_to([128, H, 2, 32])
        s3 = sk_[:, j * 64:(j + 1) * 64].rearrange("p (t f) -> p t f", t=2) \
            .unsqueeze(1).broadcast_to([128, H, 2, 32])
        o3 = outbuf[:].rearrange("p (h t f) -> p h t f", h=H, t=2)
        t23 = t2[:].rearrange("p (h t f) -> p h t f", h=H, t=2)
        s.do("vector", lambda e: e.tensor_tensor(t23[:, :, 0:1, :], qb3[:, :, 1:2, :],
                                                 s3[:, :, 0:1, :], op=ALU.mult),
             rd=[bQB, bSMALL], wr=[bT2])
        s.do("vector", lambda e: e.tensor_tensor(t23[:, :, 1:2, :], qb3[:, :, 0:1, :],
                                                 s3[:, :, 1:2, :], op=ALU.mult),
             rd=[bQB, bSMALL], wr=[bT2])
        s.do("vector", lambda e: e.tensor_tensor(o3, qb3, c3, op=ALU.mult),
             rd=[bQB, bSMALL], wr=[bout])
        s.do("vector", lambda e: e.tensor_tensor(outbuf[:], outbuf[:], t2[:], op=ALU.add),
             rd=[bT2], wr=[bout])

    def tr_tile(src, bsrc, dst_ap, bw):
        dma_T(dst_ap.rearrange("p (g f) -> p g f", g=8),
              src[:].rearrange("p (g f) -> p g f", g=8), rd=[bsrc], wr=[bw])

    def hsl(t_, h, half=None, width=128):
        p0 = (h % 2) * 64
        c0 = (h // 2) * width + (0 if half is None else half * 1024)
        return t_[p0:p0 + 64, c0:c0 + width]

    def ln1(t):
        xsl, bx = xs[t % XR], bXS[t % XR]
        ln_stats(xsl, bx, bns, bna, lnt, bLN)
        ln_apply(xsl, bx, xnst[t % 2][:], bXNS[t % 2], lnt, bLN)
        tr_tile(xnst[t % 2], bXNS[t % 2], xnT[t % 2][:], bXNT[t % 2])

    ln1(0)
    _ssem = ("lds", "out", "tr3")
    for si, (dst, src) in enumerate(((bq, bqh), (bk, bkh), (bv, bvh), (bo, boh), (bf_, bfh),
                                     (mstd, msh), (mj0, mj0h), (mA1, mA1h),
                                     (idf, idfh), (idb, idbh), (eoh, eohh), (as128, ash))):
        s.do("scalar", lambda e, d=dst, sr=src: e.dma_start(out=d[:], in_=sr),
             wr=[bSMALL], dma_sem=_ssem[si % 3])
    for si, (dst, src) in enumerate(((ck, ckh), (sk_, skh))):
        s.do("scalar", lambda e, d=dst, sr=src: e.dma_start(
            out=d[:].rearrange("p (t f) -> p t f", t=NT),
            in_=sr.rearrange("(t p) f -> p t f", p=128)),
            wr=[bSMALL], dma_sem=_ssem[si % 3])
    s.do("vector", lambda e: e.tensor_scalar_max(smax[:], as128[:], 1e-5),
         rd=[bSMALL], wr=[bSC])
    s.do("vector", lambda e: e.reciprocal(rs128[:], smax[:]), rd=[bSC], wr=[bSC])
    s.do("vector", lambda e: e.tensor_scalar_mul(bq[:], bq[:], 0.125),
         rd=[bSMALL], wr=[bSMALL])
    for hf in range(4):
        prep_quarter("wq", wq, hf)
        prep_quarter("wk", wk, hf)
        prep_quarter("wv", wv, hf)
    for hf in range(4):
        prep_quarter("wo", wo, hf)
    wstgF = expT[1]
    bSTF = [bEXPA[1], bEXPB[1]]
    for hf in range(4):
        s.do("gpsimd", lambda e, hf=hf: e.dma_start(
            out=wstgF[:, 0:2048].rearrange("p (g f) -> p g f", g=2),
            in_=tern[hf * 256:(hf + 1) * 256, :].rearrange("(g p) f -> p g f", p=128)),
            rd=bSTF, wr=bSTF, dma_sem="tr2")
        s.do("vector", lambda e: e.tensor_scalar_add(wstgF[:, 0:2048], wstgF[:, 0:2048], -1.0),
             rd=bSTF, wr=bSTF)
        for og4 in range(2):
            og = hf * 2 + og4
            s.do("sync", lambda e, og=og, og4=og4: e.dma_start_transpose(
                wT["wf"][:].rearrange("p (g f) -> p g f", g=8)[:, :, og * 128:(og + 1) * 128],
                wstgF[:, og4 * 1024:(og4 + 1) * 1024].rearrange("p (g f) -> p g f", g=8)),
                rd=bSTF, wr=[bW["wf"][hf]], dma_sem="tr1")


    for j in range(0, NT + 4):

        t = j - 1
        doA = 1 <= t <= 16
        doQ = j <= 16
        if doA:
            qd, bqdL = qTd[t % 3], bQTdL[t % 3]
            kd = kTd[t % 3]
            bkdL, bkdR = bKTdL[t % 3], bKTdR[t % 3]

            def passA_group(hg, t=t, qd=qd, kd=kd, bqdL=bqdL, bkdL=bkdL, bkdR=bkdR):
                slot = (hg % 2) * 256
                for hh in range(2):
                    h = hg * 2 + hh
                    pb_ = 6 + hh
                    rhs = kd[(h % 2) * 64:(h % 2) * 64 + 64, :] \
                        .rearrange("p (a c) -> p a c", a=2)[:, :, (h // 2) * 128:(h // 2) * 128 + 128]
                    s.do("tensor", lambda e, h=h, slot=slot, pb_=pb_, qd=qd, rhs=rhs:
                         e.matmul(bank[pb_][:, slot:slot + 256],
                                  hsl(qd, h, 0), rhs,
                                  start=True, stop=True),
                         rd=[bqdL, bkdL, bkdR], wr=[bBANK[pb_]])
                for hh in range(2):
                    h = hg * 2 + hh
                    pb_ = 6 + hh
                    if t == 1:
                        s.do("vector", lambda e, pb_=pb_, slot=slot: e.tensor_tensor(
                            bank[pb_][:, slot:slot + 256], bank[pb_][:, slot:slot + 256],
                            mA1[:], op=ALU.add),
                            rd=[bSMALL], wr=[bBANK[pb_]])
                    s.do("vector", lambda e, h=h, pb_=pb_, slot=slot: e.reduce_max(
                        nmt[:, h:h + 1], bank[pb_][:, slot:slot + 256],
                        axis=AX.X, negate=True),
                        rd=[bBANK[pb_]], wr=[bNM])
        if doQ:
            xnsl, bxnsl = xnT[j % 2], bXNT[j % 2]

            def proj_mms(wt_, bw_, pb0, pb1, inter=None):
                for dg in range(8):
                    lhs = xnsl[:, dg * 128:(dg + 1) * 128]
                    for half, pb_ in ((0, pb0), (1, pb1)):
                        s.do("tensor", lambda e, lhs=lhs, wt_=wt_, dg=dg, half=half, pb_=pb_, st=(dg == 0), sp=(dg == 7):
                             e.matmul(bank[pb_][:], lhs,
                                      wt_[:, dg * 1024 + half * 512: dg * 1024 + (half + 1) * 512],
                                      start=st, stop=sp),
                             rd=[bxnsl, bw_[2 * half], bw_[2 * half + 1]], wr=[bBANK[pb_]])
                    if inter is not None and dg % 2 == 1:
                        passA_group(inter + dg // 2)

        if doQ:
            proj_mms(wvT, bW["wv"], 4, 5, inter=(0 if doA else None))
        elif doA:
            for hg in range(4):
                passA_group(hg)
        if doQ:
            va, bva = vaug[j % 4], bVA[j % 4]
            va3 = va[:].rearrange("p (h c) -> p h c", h=H)[:, :, 0:64]
            bv3 = bv[:].rearrange("p (h c) -> p h c", h=H)
            for half, pb_ in ((0, 4), (1, 5)):
                s.do("vector", lambda e, half=half, pb_=pb_, va3=va3, bv3=bv3: e.scalar_tensor_tensor(
                    va3[:, half * 8:(half + 1) * 8, :],
                    bank[pb_][:].rearrange("p (h c) -> p h c", h=8),
                    1.0, bv3[:, half * 8:(half + 1) * 8, :],
                    op0=ALU.mult, op1=ALU.add),
                    rd=[bBANK[pb_], bSMALL], wr=[bva])
        if doQ:
            proj_mms(wqT, bW["wq"], 0, 1, inter=(4 if doA else None))
        elif doA:
            for hg in range(4):
                passA_group(4 + hg)

        if 1 <= j - 1 <= 16:
            t = j - 1
            s.do("tensor", lambda e: e.transpose(bank[6][0:16, 0:128], nmt[:], idf[:]),
                 rd=[bNM, bSMALL], wr=[bBANK[6]])
            s.do("vector", lambda e, t=t: e.tensor_copy(nmT[0:16, t * 128:(t + 1) * 128],
                                                        bank[6][0:16, 0:128]),
                 rd=[bBANK[6]], wr=[bNMT[t]])

        if doQ and j >= 1:
            rope((0, 1), bq, j, q_tm[j % 2], bQTM[j % 2])
            tr_tile(q_tm[j % 2], bQTM[j % 2], qTd[j % 3][:][:, 0:1024], bQTdL[j % 3])
            tr_tile(q_tm[j % 2], bQTM[j % 2], qTd[(j - 1) % 3][:][:, 1024:2048], bQTdR[(j - 1) % 3])

        if doQ:
            proj_mms(wkT, bW["wk"], 2, 3)
            rope((2, 3), bk, j, k_tm[j % 2], bKTM[j % 2])
            tr_tile(k_tm[j % 2], bKTM[j % 2], kTd[j % 3][:][:, 1024:2048], bKTdR[j % 3])
            if j + 1 <= 16:
                tr_tile(k_tm[j % 2], bKTM[j % 2], kTd[(j + 1) % 3][:][:, 0:1024], bKTdL[(j + 1) % 3])

        if j + 1 <= 16:
            ln1(j + 1)

        if 0 <= j - 2 <= 16:
            t = j - 2
            ebuf = expT[t % 2]
            kd = kTd[t % 3]
            bkdR = bKTdR[t % 3]
            qd, bqdL, bqdR = qTd[t % 3], bQTdL[t % 3], bQTdR[t % 3]
            first_half = t >= 1
            second_half = t <= 15
            for hg8 in (1, 0):
                pbs = (4, 5, 6, 7) if hg8 == 0 else (0, 1, 2, 3)
                beb = (bEXPA if hg8 == 0 else bEXPB)[t % 2]
                def hb(h):
                    q = (h % 8) // 2
                    return pbs[2 * (q // 2) + (h % 2)], (q % 2) * 256
                for q4 in range(4):
                    for hh in range(2):
                        h = hg8 * 8 + q4 * 2 + hh
                        pb_, slot = hb(h)
                        if first_half and second_half:
                            rhs = qd[(h % 2) * 64:(h % 2) * 64 + 64, :] \
                                .rearrange("p (a c) -> p a c", a=2)[:, :, (h // 2) * 128:(h // 2) * 128 + 128]
                            o0, cw, c0 = 0, 256, t * 128
                            rdq = [bqdL, bqdR]
                        elif second_half:
                            rhs = hsl(qd, h, 1)
                            o0, cw, c0 = 128, 128, (t + 1) * 128
                            rdq = [bqdR]
                        else:
                            rhs = hsl(qd, h, 0)
                            o0, cw, c0 = 0, 128, t * 128
                            rdq = [bqdL]
                        sc_last = s.do("tensor", lambda e, h=h, slot=slot, pb_=pb_, kd=kd, rhs=rhs, o0=o0, cw=cw, st=(slot == 0):
                             e.matmul(bank[pb_][:, slot + o0: slot + o0 + cw],
                                      hsl(kd, h, 1), rhs,
                                      start=st, stop=False),
                             rd=[bkdR] + rdq, wr=[bBANK[pb_]])
                rdl = [bSMALL]
                if first_half:
                    rdl.append(bNMT[t])
                if second_half:
                    rdl.append(bNMT[t + 1])
                for q4 in range(4):
                    for hh in range(2):
                        h = hg8 * 8 + q4 * 2 + hh
                        pb_, slot = hb(h)
                        s.do("tensor", lambda e, h=h, slot=slot, pb_=pb_, c0=c0, cw=cw, o0=o0:
                             e.matmul(bank[pb_][:, slot + o0: slot + o0 + cw],
                                      eoh[:, h * 128:(h + 1) * 128],
                                      nmT[0:16, c0:c0 + cw],
                                      start=False, stop=False),
                             rd=rdl, wr=[bBANK[pb_]],
                             keep_pe=(q4 == 0 and hh == 0), also=([sc_last] if (q4 == 0 and hh == 0) else ()))
                if first_half and second_half:
                    mrhs = mstd[:]
                elif second_half:
                    mrhs = mj0[:]
                else:
                    mrhs = mstd[:, 0:128]
                for q4 in range(4):
                    for hh in range(2):
                        h = hg8 * 8 + q4 * 2 + hh
                        pb_, slot = hb(h)
                        s.do("tensor", lambda e, pb_=pb_, slot=slot, o0=o0, cw=cw, mrhs=mrhs, sp=(slot != 0):
                             e.matmul(bank[pb_][:, slot + o0: slot + o0 + cw],
                                      idb[:], mrhs,
                                      start=False, stop=sp),
                             rd=[bSMALL], wr=[bBANK[pb_]])
                for h in range(hg8 * 8, hg8 * 8 + 8):
                    pb_, slot = hb(h)
                    if first_half and second_half:
                        s.do("scalar", lambda e, pb_=pb_, slot=slot, h=h, ebuf=ebuf: e.activation(
                            ebuf[:, h * 256:(h + 1) * 256], bank[pb_][:, slot:slot + 256], AF.Exp),
                            rd=[bBANK[pb_]], wr=[beb])
                    else:
                        sl0, sl1 = (128, 256) if second_half else (0, 128)
                        s.do("scalar", lambda e, pb_=pb_, slot=slot, h=h, sl0=sl0, sl1=sl1, ebuf=ebuf: e.activation(
                            ebuf[:, h * 256 + sl0:h * 256 + sl1],
                            bank[pb_][:, slot + sl0:slot + sl1], AF.Exp),
                            rd=[bBANK[pb_]], wr=[beb])

        if 1 <= j - 4 <= 16:
            t = j - 4
            xT_, bxT_ = xiT[t % 2], bXIT[t % 2]
            for dg in range(8):
                for half, pb_ in ((0, 0), (1, 1)):
                    s.do("tensor", lambda e, dg=dg, half=half, pb_=pb_, xT_=xT_:
                         e.matmul(bank[pb_][:], xT_[:, dg * 128:(dg + 1) * 128],
                                  wfT[:, dg * 1024 + half * 512: dg * 1024 + (half + 1) * 512],
                                  start=(dg == 0), stop=(dg == 7)),
                         rd=[bxT_, bW["wf"][2 * half], bW["wf"][2 * half + 1]], wr=[bBANK[pb_]])
            xt, bxt = xs[t % XR], bXS[t % XR]
            s.do("vector", lambda e, xt=xt: e.tensor_tensor(xt[:], xt[:], bf_[:], op=ALU.add),
                 rd=[bSMALL, bxt], wr=[bxt])
            for half, pb_ in ((0, 0), (1, 1)):
                s.do("vector", lambda e, half=half, pb_=pb_, xt=xt: e.scalar_tensor_tensor(
                    xt[:, half * 512:(half + 1) * 512], bank[pb_][:], smax[:],
                    xt[:, half * 512:(half + 1) * 512], op0=ALU.mult, op1=ALU.add),
                    rd=[bBANK[pb_], bSC], wr=[bxt])
            s.do("scalar", lambda e, t=t, xt=xt: e.dma_start(
                out=xout[(t - 1) * 128: t * 128, :], in_=xt[:]),
                rd=[bxt], dma_sem="out")

        if 1 <= j - 2 <= 16:
            t = j - 2
            em1 = expT[(t - 1) % 2]
            em0 = expT[t % 2]
            va1, bva1 = vaug[(t - 1) % 4], bVA[(t - 1) % 4]
            va0, bva0 = vaug[t % 4], bVA[t % 4]
            groups = ((4, 0, tuple(range(0, 7))), (5, 0, tuple(range(7, 14))), (7, 0, (14, 15)))
            for pb_, off, hs in groups:
                for i, h in enumerate(hs):
                    c = off + (h - hs[0]) * 65
                    b1a, b1b = bEXPA[(t - 1) % 2], bEXPB[(t - 1) % 2]
                    b0a, b0b = bEXPA[t % 2], bEXPB[t % 2]
                    s.do("tensor", lambda e, h=h, c=c, pb_=pb_, st=(i == 0), em1=em1, va1=va1:
                         e.matmul(bank[pb_][:, c:c + 65],
                                  em1[:, h * 256 + 128: h * 256 + 256],
                                  va1[:, h * 65:(h + 1) * 65],
                                  start=st, stop=False),
                         rd=[b1a, b1b, bva1], wr=[bBANK[pb_]])
                    s.do("tensor", lambda e, h=h, c=c, pb_=pb_, sp=(i == len(hs) - 1), em0=em0, va0=va0:
                         e.matmul(bank[pb_][:, c:c + 65],
                                  em0[:, h * 256: h * 256 + 128],
                                  va0[:, h * 65:(h + 1) * 65],
                                  start=False, stop=sp),
                         rd=[b0a, b0b, bva0], wr=[bBANK[pb_]])
            for pb_, off, hs in groups:
                n = len(hs)
                s.do("scalar", lambda e, pb_=pb_, off=off, h0=hs[0], n=n: e.activation(
                    den[:, h0:h0 + n],
                    bank[pb_][:, off:off + 65 * n].rearrange("p (a b) -> p a b", a=n)[:, :, 64:65].squeeze(2),
                    AF.Identity),
                    rd=[bBANK[pb_]], wr=[bDEN])
            s.do("vector", lambda e: e.reciprocal(rec[:], den[:]), rd=[bDEN], wr=[bDEN])
            at, bat = attn[t % 2], bATTN[t % 2]
            for pb_, off, hs in groups:
                for h in hs:
                    c = off + (h - hs[0]) * 65
                    s.do("scalar", lambda e, h=h, c=c, pb_=pb_, at=at: e.activation(
                        at[:, h * 64:(h + 1) * 64], bank[pb_][:, c:c + 64], AF.Identity,
                        scale=rec[:, h:h + 1]),
                        rd=[bBANK[pb_], bDEN], wr=[bat])

        if 1 <= j - 3 <= 16:
            t = j - 3
            aT, baT = attT[t % 2], bATTT[t % 2]
            for dg in range(8):
                for half, pb_ in ((0, 1), (1, 2)):
                    s.do("tensor", lambda e, dg=dg, half=half, pb_=pb_, aT=aT:
                         e.matmul(bank[pb_][:], aT[:, dg * 128:(dg + 1) * 128],
                                  woT[:, dg * 1024 + half * 512: dg * 1024 + (half + 1) * 512],
                                  start=(dg == 0), stop=(dg == 7)),
                         rd=[baT, bW["wo"][2 * half], bW["wo"][2 * half + 1]], wr=[bBANK[pb_]])
            xt, bxt = xs[t % XR], bXS[t % XR]
            s.do("vector", lambda e, xt=xt: e.tensor_tensor(xt[:], xt[:], bo[:], op=ALU.add),
                 rd=[bSMALL, bxt], wr=[bxt])
            for half, pb_ in ((0, 1), (1, 2)):
                s.do("vector", lambda e, half=half, pb_=pb_, xt=xt: e.scalar_tensor_tensor(
                    xt[:, half * 512:(half + 1) * 512], bank[pb_][:], 1.0,
                    xt[:, half * 512:(half + 1) * 512], op0=ALU.mult, op1=ALU.add),
                    rd=[bBANK[pb_]], wr=[bxt])
            ln_stats(xt, bxt, bns2, bna2, lnt2, bLN2)
            ln_apply(xt, bxt, xn2[:], bXN2, lnt2, bLN2)
            s.do("vector", lambda e: e.tensor_scalar(xn2[:], xn2[:], rs128[:], None, op0=ALU.mult),
                 rd=[bSC], wr=[bXN2])
            s.do("vector", lambda e: e.tensor_scalar(xn2[:], xn2[:], MAGIC, MAGIC,
                                                     op0=ALU.add, op1=ALU.subtract),
                 rd=[], wr=[bXN2])
            s.do("vector", lambda e, t=t: e.tensor_scalar(xint[t % 2][:], xn2[:], -127.0, 127.0,
                                                          op0=ALU.max, op1=ALU.min),
                 rd=[bXN2], wr=[bXINT[t % 2]])
            tr_tile(xint[t % 2], bXINT[t % 2], xiT[t % 2][:], bXIT[t % 2])

        if 1 <= j - 2 <= 16:
            ta_ = j - 2
            tr_tile(attn[ta_ % 2], bATTN[ta_ % 2], attT[ta_ % 2][:], bATTT[ta_ % 2])

        if 2 <= j + 2 <= NT - 1:
            s.do("sync", lambda e, t=j + 2: e.dma_start(
                out=xs[t % XR][:], in_=xin[t * 128:(t + 1) * 128, :]),
                wr=[bXS[(j + 2) % XR]], dma_sem="ldx")

    final_out = s.last_of_dma["out"]

    with (
        nc.semaphore("pe") as sem_pe,
        nc.semaphore("dve") as sem_dve,
        nc.semaphore("act") as sem_act,
        nc.semaphore("spq") as sem_spq,
        nc.semaphore("gpq") as sem_gpq,
        nc.semaphore("lds") as sem_lds,
        nc.semaphore("ldw") as sem_ldw,
        nc.semaphore("ldx") as sem_ldx,
        nc.semaphore("wt") as sem_wt,
        nc.semaphore("outq") as sem_out,
        nc.semaphore("tr1") as sem_tr1,
        nc.semaphore("tr2") as sem_tr2,
        nc.semaphore("tr3") as sem_tr3,
        nc.Block() as block,
    ):
        sems = {"pe": sem_pe, "dve": sem_dve, "act": sem_act,
                "spq": sem_spq, "gpq": sem_gpq, "lds": sem_lds,
                "ldw": sem_ldw, "ldx": sem_ldx, "wt": sem_wt, "out": sem_out,
                "tr1": sem_tr1, "tr2": sem_tr2, "tr3": sem_tr3}
        s.ops.append(("sync", lambda e: e.nop(), {final_out}, None, False))
        s.emit(block, sems)

    for t_ in reversed(ctx):
        t_.__exit__(None, None, None)
    return nc


def _tables(h):
    if h == 0:
        pos = np.concatenate([np.zeros(128), np.arange(0, SC)])
    else:
        pos = np.arange(SC - 128, SC + SC)
    pos = pos.astype(np.float64)
    inv = 1.0 / (10000.0 ** (np.arange(32, dtype=np.float64) / 32.0))
    ang = pos[:, None] * inv[None, :]
    c = np.cos(ang).astype(np.float32)
    sn = np.sin(ang).astype(np.float32)
    c64 = np.concatenate([c, c], axis=1)
    s64 = np.concatenate([-sn, sn], axis=1)
    return c64, s64


def _eoh2():
    one = np.concatenate([np.tile((np.arange(16) == h).astype(np.float32)[:, None], (1, 128))
                          for h in range(16)], axis=1)
    e = np.zeros((80, 16 * 128), np.float32)
    e[0:16] = one
    e[64:80] = one
    return e.astype(ml_dtypes.bfloat16)


def _masks():
    p = np.arange(128)[:, None]
    c = np.arange(128)[None, :]
    blk = np.float32(-1e30)
    diag = np.where(c >= p, np.float32(0), blk)
    left = np.where(c <= p, np.float32(0), blk)
    mstd = np.concatenate([diag, left], axis=1)
    return mstd, left


def kernel(L0_attn_x, L0_attn_Wq, L0_attn_bq, L0_attn_Wk, L0_attn_bk,
           L0_attn_Wv, L0_attn_bv, L0_attn_Wo, L0_attn_bo,
           L0_ffn_ternary_weight, L0_ffn_bias, L0_ffn_act_scale):
    bfdt = ml_dtypes.bfloat16
    x = np.ascontiguousarray(np.asarray(L0_attn_x, dtype=np.float32))
    mstd, left = _masks()
    nc = build_nc()
    in_maps = []
    bcast = lambda v: np.ascontiguousarray(
        np.broadcast_to(np.asarray(v, np.float32)[None, :], (128, D)).astype(bfdt))
    common = {
        "wq": np.ascontiguousarray(np.asarray(L0_attn_Wq, np.float32)),
        "wk": np.ascontiguousarray(np.asarray(L0_attn_Wk, np.float32)),
        "wv": np.ascontiguousarray(np.asarray(L0_attn_Wv, np.float32)),
        "wo": np.ascontiguousarray(np.asarray(L0_attn_Wo, np.float32)),
        "tern": np.ascontiguousarray(np.asarray(L0_ffn_ternary_weight, np.int32)),
        "bqh": bcast(L0_attn_bq), "bkh": bcast(L0_attn_bk),
        "bvh": bcast(L0_attn_bv), "boh": bcast(L0_attn_bo),
        "bfh": bcast(L0_ffn_bias),
        "ash": np.full((128, 1), float(np.asarray(L0_ffn_act_scale)), np.float32),
        "msh": mstd.astype(bfdt),
        "idfh": np.eye(128, dtype=np.float32),
        "idbh": np.eye(128, dtype=np.float32).astype(bfdt),
        "eohh": _eoh2()[0:16],
    }
    tabs = {}
    for h in (0, 1):
        c64, s64 = _tables(h)
        tabs[h] = {"ckh": c64.astype(bfdt), "skh": s64.astype(bfdt)}
    mA1_h0 = np.concatenate([np.full((128, 128), -1e30, np.float32),
                             np.zeros((128, 128), np.float32)], axis=1)
    for core in range(8):
        b, h = core // 2, core % 2
        if h == 0:
            halo = np.zeros((128, D), np.float32)
        else:
            halo = x[b, SC - 128:SC, :]
        xin = np.ascontiguousarray(np.concatenate([halo, x[b, h * SC:(h + 1) * SC, :]], axis=0))
        m = dict(common)
        m["xin"] = xin
        m.update(tabs[h])
        m["mj0h"] = (left if h == 1 else np.full((128, 128), -1e30, np.float32)).astype(bfdt)
        m["mA1h"] = (mA1_h0 if h == 0 else np.zeros((128, 256), np.float32)).astype(bfdt)
        in_maps.append(m)
    res = run_bass_kernel_spmd(nc, in_maps, core_ids=list(range(8)))
    out = np.empty((B, S, D), np.float32)
    for core in range(8):
        b, h = core // 2, core % 2
        out[b, h * SC:(h + 1) * SC, :] = res.results[core]["xout"]
    return out



# version 25
# speedup vs baseline: 1.1751x; 1.1751x over previous
import numpy as np
import ml_dtypes
import concourse.bass as bass
import concourse.mybir as mybir
from concourse.bass_utils import run_bass_kernel_spmd

F32 = mybir.dt.float32
BF16 = mybir.dt.bfloat16
I32 = mybir.dt.int32
AF = mybir.ActivationFunctionType
ALU = mybir.AluOpType
AX = mybir.AxisListType

B, S, D = 4, 4096, 1024
H, DH = 16, 64
SC = 2048
NT = 17
MAGIC = 12582912.0
EPS = 1e-5


class Buf:
    __slots__ = ("w", "r")

    def __init__(self):
        self.w = None
        self.r = []


class Sched:
    ENGINES = ("tensor", "vector", "scalar", "sync", "gpsimd")
    DMA_SEMS = ("lds", "ldw", "ldx", "wt", "out", "tr1", "tr2", "tr3")

    def __init__(self):
        self.ops = []
        self.last_of_dma = {d: None for d in self.DMA_SEMS}
        self.own = {"tensor": "pe", "vector": "dve", "scalar": "act",
                    "sync": "spq", "gpsimd": "gpq"}

    def do(self, engine, emit, rd=(), wr=(), dma_sem=None, keep_pe=False, also=()):
        oid = len(self.ops)
        deps = set(also)
        if dma_sem is not None and self.last_of_dma[dma_sem] is not None:
            deps.add(self.last_of_dma[dma_sem])
        for b in rd:
            if b.w is not None:
                deps.add(b.w)
        for b in wr:
            if b.w is not None:
                deps.add(b.w)
            deps.update(b.r)
        self.ops.append((engine, emit, deps, dma_sem, keep_pe))
        if dma_sem is not None:
            self.last_of_dma[dma_sem] = oid
        for b in rd:
            b.r.append(oid)
        for b in wr:
            b.w = oid
            b.r = []
        return oid

    def emit(self, block, sems, final_waits=()):
        n = len(self.ops)
        engine_of = [o[0] for o in self.ops]
        dma_of = [o[3] for o in self.ops]
        keep_of = [o[4] for o in self.ops]
        sem_of = [dma_of[i] if dma_of[i] else self.own[engine_of[i]]
                  for i in range(n)]
        awaited = [False] * n
        for i, (eng, _e, deps, _d, _k) in enumerate(self.ops):
            for d in deps:
                if engine_of[i] == "tensor" and engine_of[d] == "tensor" \
                        and dma_of[d] is None and not keep_of[i]:
                    continue
                awaited[d] = True
        for d in final_waits:
            awaited[d] = True
        val = [None] * n
        cum = {}
        for i in range(n):
            if dma_of[i] is not None:
                awaited[i] = True
            if awaited[i]:
                sm = sem_of[i]
                cum[sm] = cum.get(sm, 0) + (16 if dma_of[i] else 1)
                val[i] = cum[sm]
        qs = {e: [] for e in self.ENGINES}
        last_wait = {e: {} for e in self.ENGINES}
        for i, (eng, emitfn, deps, dma, _k) in enumerate(self.ops):
            need = {}
            for d in deps:
                if engine_of[i] == "tensor" and engine_of[d] == "tensor" \
                        and dma_of[d] is None and not keep_of[i]:
                    continue
                sm, v = sem_of[d], val[d]
                assert v is not None
                if need.get(sm, 0) < v:
                    need[sm] = v
            waits = []
            lw = last_wait[eng]
            for sm, v in need.items():
                if lw.get(sm, 0) < v:
                    waits.append((sm, v))
                    lw[sm] = v
            inc = (sem_of[i], 16 if dma else 1) if awaited[i] else None
            qs[eng].append((waits, emitfn, inc))

        def mk(ops):
            def _(eng):
                for waits, emitfn, inc in ops:
                    for sm, v in waits:
                        eng.wait_ge(sems[sm], v)
                    instr = emitfn(eng)
                    if inc is not None:
                        instr.then_inc(sems[inc[0]], inc[1])
            return _
        for e in self.ENGINES:
            if qs[e]:
                getattr(block, e)(mk(qs[e]))
        return {d: (sem_of[d], val[d]) for d in final_waits}


def build_nc():
    nc = bass.Bass("TRN2", target_bir_lowering=False)

    xin = nc.dram_tensor("xin", [NT * 128, D], F32, kind="ExternalInput").ap()
    wq = nc.dram_tensor("wq", [D, D], F32, kind="ExternalInput").ap()
    wk = nc.dram_tensor("wk", [D, D], F32, kind="ExternalInput").ap()
    wv = nc.dram_tensor("wv", [D, D], F32, kind="ExternalInput").ap()
    wo = nc.dram_tensor("wo", [D, D], F32, kind="ExternalInput").ap()
    tern = nc.dram_tensor("tern", [D, D], I32, kind="ExternalInput").ap()
    bqh = nc.dram_tensor("bqh", [128, D], BF16, kind="ExternalInput").ap()
    bkh = nc.dram_tensor("bkh", [128, D], BF16, kind="ExternalInput").ap()
    bvh = nc.dram_tensor("bvh", [128, D], BF16, kind="ExternalInput").ap()
    boh = nc.dram_tensor("boh", [128, D], BF16, kind="ExternalInput").ap()
    bfh = nc.dram_tensor("bfh", [128, D], BF16, kind="ExternalInput").ap()
    ash = nc.dram_tensor("ash", [128, 1], F32, kind="ExternalInput").ap()
    ckh = nc.dram_tensor("ckh", [NT * 128, DH], BF16, kind="ExternalInput").ap()
    skh = nc.dram_tensor("skh", [NT * 128, DH], BF16, kind="ExternalInput").ap()
    msh = nc.dram_tensor("msh", [128, 256], BF16, kind="ExternalInput").ap()
    mj0h = nc.dram_tensor("mj0h", [128, 128], BF16, kind="ExternalInput").ap()
    mA1h = nc.dram_tensor("mA1h", [128, 256], BF16, kind="ExternalInput").ap()
    idfh = nc.dram_tensor("idfh", [128, 128], F32, kind="ExternalInput").ap()
    idbh = nc.dram_tensor("idbh", [128, 128], BF16, kind="ExternalInput").ap()
    eohh = nc.dram_tensor("eohh", [16, 16 * 128], BF16, kind="ExternalInput").ap()
    xout = nc.dram_tensor("xout", [SC, D], F32, kind="ExternalOutput").ap()

    s = Sched()
    ctx = []

    def sb(name, shape, dt):
        t = nc.sbuf_tensor(name, shape, dt)
        ctx.append(t)
        return t.__enter__()

    def ps(name, shape, dt):
        t = nc.psum_tensor(name, shape, dt)
        ctx.append(t)
        return t.__enter__()

    XR = 6
    xs = [sb(f"xs{i}", [128, D], F32) for i in range(XR)]
    xnst = [sb("xnst0", [128, D], BF16)] * 2
    xnT = [sb(f"xnT{i}", [128, D], BF16) for i in range(2)]
    qb = sb("qb", [128, D], BF16)
    t2 = sb("t2", [128, D], BF16)
    q_tm = [sb("qtm0", [128, D], BF16)] * 2
    k_tm = [sb("ktm0", [128, D], BF16)] * 2
    qTd = [sb(f"qTd{i}", [128, 2 * D], BF16) for i in range(3)]
    kTd = [sb(f"kTd{i}", [128, 2 * D], BF16) for i in range(3)]
    vaug = [sb(f"vaug{i}", [128, H * 65], BF16) for i in range(4)]
    expT = [sb(f"expT{i}", [128, H * 256], BF16) for i in range(2)]
    attn = [sb("attn0", [128, D], BF16)] * 2
    attT = [sb("attT0", [128, D], BF16)] * 2
    xn2 = sb("xn2", [128, D], F32)
    xint = [sb("xint0", [128, D], BF16)] * 2
    xiT = [sb("xiT0", [128, D], BF16)] * 2
    wqT = sb("wqT", [128, 8 * D], BF16)
    wkT = sb("wkT", [128, 8 * D], BF16)
    wvT = sb("wvT", [128, 8 * D], BF16)
    woT = sb("woT", [128, 8 * D], BF16)
    wfT = sb("wfT", [128, 8 * D], BF16)
    bq = sb("bq", [128, D], BF16)
    bk = sb("bk", [128, D], BF16)
    bv = sb("bv", [128, D], BF16)
    bo = sb("bo", [128, D], BF16)
    bf_ = sb("bf_", [128, D], BF16)
    as128 = sb("as128", [128, 1], F32)
    smax = sb("smax", [128, 1], F32)
    rs128 = sb("rs128", [128, 1], F32)
    ck = sb("ck", [128, NT * DH], BF16)
    sk_ = sb("sk_", [128, NT * DH], BF16)
    mstd = sb("mstd", [128, 256], BF16)
    mj0 = sb("mj0", [128, 128], BF16)
    mA1 = sb("mA1", [128, 256], BF16)
    idf = sb("idf", [128, 128], F32)
    idb = sb("idb", [128, 128], BF16)
    eoh = sb("eoh", [16, 16 * 128], BF16)
    nmT = sb("nmT", [16, NT * 128], BF16)
    nmt = sb("nmt", [128, 16], F32)
    bna = sb("bna", [128, 2], F32)
    bns = sb("bns", [128, 12], F32)
    lnt = sb("lnt", [128, 4], F32)
    bna2 = sb("bna2", [128, 2], F32)
    bns2 = sb("bns2", [128, 12], F32)
    lnt2 = sb("lnt2", [128, 4], F32)
    den = sb("den", [128, 16], F32)
    rec = sb("rec", [128, 16], F32)
    cst = sb("cst", [128, 2], F32)
    wstg = sb("wstg", [128, 2 * D], BF16)
    nc.const_aps.register(F32, EPS, cst[:, 0:1])
    nc.const_aps.register(F32, -0.5, cst[:, 1:2])

    bank = [ps(f"pb{i}", [128, 512], F32) for i in range(8)]

    bXS = [Buf() for _ in range(XR)]
    _bxn = Buf(); bXNS = [_bxn, _bxn]
    bXNT = [Buf() for _ in range(2)]
    bQB, bT2 = Buf(), Buf()
    _bq = Buf(); _bk = Buf()
    bQTM = [_bq, _bq]
    bKTM = [_bk, _bk]
    bQTdL = [Buf() for _ in range(3)]
    bQTdR = [Buf() for _ in range(3)]
    bKTdL = [Buf() for _ in range(3)]
    bKTdR = [Buf() for _ in range(3)]
    bVA = [Buf() for _ in range(4)]
    bEXPA = [Buf() for _ in range(2)]
    bEXPB = [Buf() for _ in range(2)]
    _ba = Buf(); bATTN = [_ba, _ba]
    _bat = Buf(); bATTT = [_bat, _bat]
    bXN2 = Buf()
    _bxi = Buf(); bXINT = [_bxi, _bxi]
    _bxt2 = Buf(); bXIT = [_bxt2, _bxt2]
    bW = {n: [Buf() for _ in range(4)] for n in ("wq", "wk", "wv", "wo", "wf")}
    bWSTG = Buf()
    bSMALL = Buf()
    bNMT = [Buf() for _ in range(NT)]
    bNM = Buf()
    bLN = Buf()
    bLN2 = Buf()
    bDEN = Buf()
    bCONST = Buf()
    bBANK = [Buf() for _ in range(8)]
    bSC = Buf()

    wT = {"wq": wqT, "wk": wkT, "wv": wvT, "wo": woT, "wf": wfT}

    trn = [0]

    def dma_T(dst_ap, src_ap, rd, wr):
        sem = ("tr1", "tr1", "tr1")[trn[0] % 3]
        eng = "scalar" if sem == "tr3" else "sync"
        trn[0] += 1
        s.do(eng, lambda e, d=dst_ap, sr=src_ap: e.dma_start_transpose(d, sr),
             rd=rd, wr=wr, dma_sem=sem)


    s.do("vector", lambda e: e.memset(cst[:, 0:1], EPS), wr=[bCONST])
    s.do("vector", lambda e: e.memset(cst[:, 1:2], -0.5), wr=[bCONST])
    s.do("vector", lambda e: e.memset(nmT[:], 0.0), wr=[bNMT[0]])
    for i in range(4):
        s.do("vector", lambda e, i=i: e.memset(
            vaug[i][:].rearrange("p (h c) -> p h c", h=H)[:, :, 64:65], 1.0),
            wr=[bVA[i]])

    qctr = [0]

    def prep_quarter(name, src, hf):
        dst = wT[name]
        qi = qctr[0]
        qctr[0] += 1
        if qi % 2 == 0:
            stg = [wstg[:, 0:1024], wstg[:, 1024:2048]]
            bstg = [bWSTG, bWSTG]
        else:
            stg = [expT[1][:, 2048:3072], expT[1][:, 3072:4096]]
            bstg = [bEXPB[1], bEXPB[1]]
        for g in range(2):
            s.do("scalar", lambda e, hf=hf, g=g, src=src: e.dma_start(
                out=xs[4 + g][:],
                in_=src[hf * 256 + g * 128: hf * 256 + (g + 1) * 128, :]),
                wr=[bXS[4 + g]], dma_sem=("wt", "ldw")[(2 * qi + g) % 2])
        for g in range(2):
            if name == "wq":
                s.do("vector", lambda e, g=g, stg=stg: e.tensor_scalar_mul(
                    stg[g], xs[4 + g][:], 0.125),
                    rd=[bXS[4 + g]], wr=[bstg[g]])
            else:
                s.do("vector", lambda e, g=g, stg=stg: e.tensor_copy(
                    stg[g], xs[4 + g][:]),
                    rd=[bXS[4 + g]], wr=[bstg[g]])
        for og4 in range(2):
            og = hf * 2 + og4
            s.do("sync", lambda e, og=og, og4=og4, dst=dst, stg=stg: e.dma_start_transpose(
                dst[:].rearrange("p (g f) -> p g f", g=8)[:, :, og * 128:(og + 1) * 128],
                stg[og4].rearrange("p (g f) -> p g f", g=8)),
                rd=[bstg[og4]], wr=[bW[name][hf]], dma_sem="tr1")

    s.do("sync", lambda e: e.dma_start(out=xs[0][:], in_=xin[0:128, :]),
         wr=[bXS[0]], dma_sem="ldx")
    s.do("sync", lambda e: e.dma_start(out=xs[1][:], in_=xin[128:256, :]),
         wr=[bXS[1]], dma_sem="ldx")
    def ln_stats(xbuf, bx, bnsX, bnaX, lntX, bL):
        s.do("vector", lambda e: e.bn_stats(bnsX[:, 0:6], xbuf[:, 0:512]), rd=[bx], wr=[bL])
        s.do("vector", lambda e: e.bn_stats(bnsX[:, 6:12], xbuf[:, 512:1024]), rd=[bx], wr=[bL])
        s.do("vector", lambda e: e.bn_aggr(bnaX[:], bnsX[:]), rd=[bL], wr=[bL])
        s.do("scalar", lambda e: e.activation(lntX[:, 2:3], bnaX[:, 1:2], AF.Ln, bias=EPS),
             rd=[bL, bCONST], wr=[bL])
        s.do("scalar", lambda e: e.activation(lntX[:, 0:1], lntX[:, 2:3], AF.Exp, scale=-0.5),
             rd=[bL, bCONST], wr=[bL])
        s.do("vector", lambda e: e.tensor_scalar(lntX[:, 1:2], bnaX[:, 0:1], lntX[:, 0:1],
                                                 -1.0, op0=ALU.mult, op1=ALU.mult),
             rd=[bL], wr=[bL])

    def ln_apply(xbuf, bx, out, bout, lntX, bL):
        s.do("scalar", lambda e: e.activation(out, xbuf[:], AF.Identity,
                                              bias=lntX[:, 1:2], scale=lntX[:, 0:1]),
             rd=[bx, bL], wr=[bout])

    def rope(proj_banks, bias_sb, j, outbuf, bout):
        pb0, pb1 = proj_banks
        s.do("vector", lambda e: e.scalar_tensor_tensor(
            qb[:, 0:512], bank[pb0][:], 1.0, bias_sb[:, 0:512], op0=ALU.mult, op1=ALU.add),
            rd=[bBANK[pb0], bSMALL], wr=[bQB])
        s.do("vector", lambda e: e.scalar_tensor_tensor(
            qb[:, 512:1024], bank[pb1][:], 1.0, bias_sb[:, 512:1024], op0=ALU.mult, op1=ALU.add),
            rd=[bBANK[pb1], bSMALL], wr=[bQB])
        qb3 = qb[:].rearrange("p (h t f) -> p h t f", h=H, t=2)
        c3 = ck[:, j * 64:(j + 1) * 64].rearrange("p (t f) -> p t f", t=2) \
            .unsqueeze(1).broadcast_to([128, H, 2, 32])
        s3 = sk_[:, j * 64:(j + 1) * 64].rearrange("p (t f) -> p t f", t=2) \
            .unsqueeze(1).broadcast_to([128, H, 2, 32])
        o3 = outbuf[:].rearrange("p (h t f) -> p h t f", h=H, t=2)
        t23 = t2[:].rearrange("p (h t f) -> p h t f", h=H, t=2)
        s.do("vector", lambda e: e.tensor_tensor(t23[:, :, 0:1, :], qb3[:, :, 1:2, :],
                                                 s3[:, :, 0:1, :], op=ALU.mult),
             rd=[bQB, bSMALL], wr=[bT2])
        s.do("vector", lambda e: e.tensor_tensor(t23[:, :, 1:2, :], qb3[:, :, 0:1, :],
                                                 s3[:, :, 1:2, :], op=ALU.mult),
             rd=[bQB, bSMALL], wr=[bT2])
        s.do("vector", lambda e: e.tensor_tensor(o3, qb3, c3, op=ALU.mult),
             rd=[bQB, bSMALL], wr=[bout])
        s.do("vector", lambda e: e.tensor_tensor(outbuf[:], outbuf[:], t2[:], op=ALU.add),
             rd=[bT2], wr=[bout])

    def tr_tile(src, bsrc, dst_ap, bw):
        dma_T(dst_ap.rearrange("p (g f) -> p g f", g=8),
              src[:].rearrange("p (g f) -> p g f", g=8), rd=[bsrc], wr=[bw])

    def hsl(t_, h, half=None, width=128):
        p0 = (h % 2) * 64
        c0 = (h // 2) * width + (0 if half is None else half * 1024)
        return t_[p0:p0 + 64, c0:c0 + width]

    def ln1(t):
        xsl, bx = xs[t % XR], bXS[t % XR]
        ln_stats(xsl, bx, bns, bna, lnt, bLN)
        ln_apply(xsl, bx, xnst[t % 2][:], bXNS[t % 2], lnt, bLN)
        tr_tile(xnst[t % 2], bXNS[t % 2], xnT[t % 2][:], bXNT[t % 2])

    ln1(0)
    _ssem = ("lds", "out", "tr3")
    for si, (dst, src) in enumerate(((bq, bqh), (bk, bkh), (bv, bvh), (bo, boh), (bf_, bfh),
                                     (mstd, msh), (mj0, mj0h), (mA1, mA1h),
                                     (idf, idfh), (idb, idbh), (eoh, eohh), (as128, ash))):
        s.do("scalar", lambda e, d=dst, sr=src: e.dma_start(out=d[:], in_=sr),
             wr=[bSMALL], dma_sem=_ssem[si % 3])
    for si, (dst, src) in enumerate(((ck, ckh), (sk_, skh))):
        s.do("scalar", lambda e, d=dst, sr=src: e.dma_start(
            out=d[:].rearrange("p (t f) -> p t f", t=NT),
            in_=sr.rearrange("(t p) f -> p t f", p=128)),
            wr=[bSMALL], dma_sem=_ssem[si % 3])
    s.do("vector", lambda e: e.tensor_scalar_max(smax[:], as128[:], 1e-5),
         rd=[bSMALL], wr=[bSC])
    s.do("vector", lambda e: e.reciprocal(rs128[:], smax[:]), rd=[bSC], wr=[bSC])
    s.do("vector", lambda e: e.tensor_scalar_mul(bq[:], bq[:], 0.125),
         rd=[bSMALL], wr=[bSMALL])
    for hf in range(4):
        prep_quarter("wq", wq, hf)
        prep_quarter("wk", wk, hf)
        prep_quarter("wv", wv, hf)
    for hf in range(4):
        prep_quarter("wo", wo, hf)
    wstgF = expT[1]
    bSTF = [bEXPA[1], bEXPB[1]]
    for hf in range(4):
        s.do("gpsimd", lambda e, hf=hf: e.dma_start(
            out=wstgF[:, 0:2048].rearrange("p (g f) -> p g f", g=2),
            in_=tern[hf * 256:(hf + 1) * 256, :].rearrange("(g p) f -> p g f", p=128)),
            rd=bSTF, wr=bSTF, dma_sem="tr2")
        s.do("vector", lambda e: e.tensor_scalar_add(wstgF[:, 0:2048], wstgF[:, 0:2048], -1.0),
             rd=bSTF, wr=bSTF)
        for og4 in range(2):
            og = hf * 2 + og4
            s.do("sync", lambda e, og=og, og4=og4: e.dma_start_transpose(
                wT["wf"][:].rearrange("p (g f) -> p g f", g=8)[:, :, og * 128:(og + 1) * 128],
                wstgF[:, og4 * 1024:(og4 + 1) * 1024].rearrange("p (g f) -> p g f", g=8)),
                rd=bSTF, wr=[bW["wf"][hf]], dma_sem="tr1")


    for j in range(0, NT + 4):

        t = j - 1
        doA = 1 <= t <= 16
        doQ = j <= 16
        if doA:
            qd, bqdL = qTd[t % 3], bQTdL[t % 3]
            kd = kTd[t % 3]
            bkdL, bkdR = bKTdL[t % 3], bKTdR[t % 3]

            def passA_group(hg, t=t, qd=qd, kd=kd, bqdL=bqdL, bkdL=bkdL, bkdR=bkdR):
                slot = (hg % 2) * 256
                for hh in range(2):
                    h = hg * 2 + hh
                    pb_ = 6 + hh
                    rhs = kd[(h % 2) * 64:(h % 2) * 64 + 64, :] \
                        .rearrange("p (a c) -> p a c", a=2)[:, :, (h // 2) * 128:(h // 2) * 128 + 128]
                    s.do("tensor", lambda e, h=h, slot=slot, pb_=pb_, qd=qd, rhs=rhs:
                         e.matmul(bank[pb_][:, slot:slot + 256],
                                  hsl(qd, h, 0), rhs,
                                  start=True, stop=True),
                         rd=[bqdL, bkdL, bkdR], wr=[bBANK[pb_]])
                for hh in range(2):
                    h = hg * 2 + hh
                    pb_ = 6 + hh
                    if t == 1:
                        s.do("vector", lambda e, pb_=pb_, slot=slot: e.tensor_tensor(
                            bank[pb_][:, slot:slot + 256], bank[pb_][:, slot:slot + 256],
                            mA1[:], op=ALU.add),
                            rd=[bSMALL], wr=[bBANK[pb_]])
                    s.do("vector", lambda e, h=h, pb_=pb_, slot=slot: e.reduce_max(
                        nmt[:, h:h + 1], bank[pb_][:, slot:slot + 256],
                        axis=AX.X, negate=True),
                        rd=[bBANK[pb_]], wr=[bNM])
        if doQ:
            xnsl, bxnsl = xnT[j % 2], bXNT[j % 2]

            def proj_mms(wt_, bw_, pb0, pb1, inter=None):
                for dg in range(8):
                    lhs = xnsl[:, dg * 128:(dg + 1) * 128]
                    for half, pb_ in ((0, pb0), (1, pb1)):
                        s.do("tensor", lambda e, lhs=lhs, wt_=wt_, dg=dg, half=half, pb_=pb_, st=(dg == 0), sp=(dg == 7):
                             e.matmul(bank[pb_][:], lhs,
                                      wt_[:, dg * 1024 + half * 512: dg * 1024 + (half + 1) * 512],
                                      start=st, stop=sp),
                             rd=[bxnsl, bw_[2 * half], bw_[2 * half + 1]], wr=[bBANK[pb_]])
                    if inter is not None and dg % 2 == 1:
                        passA_group(inter + dg // 2)

        if doQ:
            proj_mms(wvT, bW["wv"], 4, 5, inter=(0 if doA else None))
        elif doA:
            for hg in range(4):
                passA_group(hg)
        if doQ:
            va, bva = vaug[j % 4], bVA[j % 4]
            va3 = va[:].rearrange("p (h c) -> p h c", h=H)[:, :, 0:64]
            bv3 = bv[:].rearrange("p (h c) -> p h c", h=H)
            for half, pb_ in ((0, 4), (1, 5)):
                s.do("vector", lambda e, half=half, pb_=pb_, va3=va3, bv3=bv3: e.scalar_tensor_tensor(
                    va3[:, half * 8:(half + 1) * 8, :],
                    bank[pb_][:].rearrange("p (h c) -> p h c", h=8),
                    1.0, bv3[:, half * 8:(half + 1) * 8, :],
                    op0=ALU.mult, op1=ALU.add),
                    rd=[bBANK[pb_], bSMALL], wr=[bva])
        if doQ:
            proj_mms(wqT, bW["wq"], 0, 1, inter=(4 if doA else None))
        elif doA:
            for hg in range(4):
                passA_group(4 + hg)

        if 1 <= j - 1 <= 16:
            t = j - 1
            s.do("tensor", lambda e: e.transpose(bank[6][0:16, 0:128], nmt[:], idf[:]),
                 rd=[bNM, bSMALL], wr=[bBANK[6]])
            s.do("vector", lambda e, t=t: e.tensor_copy(nmT[0:16, t * 128:(t + 1) * 128],
                                                        bank[6][0:16, 0:128]),
                 rd=[bBANK[6]], wr=[bNMT[t]])

        if doQ and j >= 1:
            rope((0, 1), bq, j, q_tm[j % 2], bQTM[j % 2])
            tr_tile(q_tm[j % 2], bQTM[j % 2], qTd[j % 3][:][:, 0:1024], bQTdL[j % 3])
            tr_tile(q_tm[j % 2], bQTM[j % 2], qTd[(j - 1) % 3][:][:, 1024:2048], bQTdR[(j - 1) % 3])

        if doQ:
            proj_mms(wkT, bW["wk"], 2, 3)
            rope((2, 3), bk, j, k_tm[j % 2], bKTM[j % 2])
            tr_tile(k_tm[j % 2], bKTM[j % 2], kTd[j % 3][:][:, 1024:2048], bKTdR[j % 3])
            if j + 1 <= 16:
                tr_tile(k_tm[j % 2], bKTM[j % 2], kTd[(j + 1) % 3][:][:, 0:1024], bKTdL[(j + 1) % 3])

        if j + 1 <= 16:
            ln1(j + 1)

        if 0 <= j - 2 <= 16:
            t = j - 2
            ebuf = expT[t % 2]
            kd = kTd[t % 3]
            bkdR = bKTdR[t % 3]
            qd, bqdL, bqdR = qTd[t % 3], bQTdL[t % 3], bQTdR[t % 3]
            first_half = t >= 1
            second_half = t <= 15
            for hg8 in (1, 0):
                pbs = (4, 5, 6, 7) if hg8 == 0 else (0, 1, 2, 3)
                beb = (bEXPA if hg8 == 0 else bEXPB)[t % 2]
                def hb(h):
                    q = (h % 8) // 2
                    return pbs[2 * (q // 2) + (h % 2)], (q % 2) * 256
                for q4 in range(4):
                    for hh in range(2):
                        h = hg8 * 8 + q4 * 2 + hh
                        pb_, slot = hb(h)
                        if first_half and second_half:
                            rhs = qd[(h % 2) * 64:(h % 2) * 64 + 64, :] \
                                .rearrange("p (a c) -> p a c", a=2)[:, :, (h // 2) * 128:(h // 2) * 128 + 128]
                            o0, cw, c0 = 0, 256, t * 128
                            rdq = [bqdL, bqdR]
                        elif second_half:
                            rhs = hsl(qd, h, 1)
                            o0, cw, c0 = 128, 128, (t + 1) * 128
                            rdq = [bqdR]
                        else:
                            rhs = hsl(qd, h, 0)
                            o0, cw, c0 = 0, 128, t * 128
                            rdq = [bqdL]
                        sc_last = s.do("tensor", lambda e, h=h, slot=slot, pb_=pb_, kd=kd, rhs=rhs, o0=o0, cw=cw, st=(slot == 0):
                             e.matmul(bank[pb_][:, slot + o0: slot + o0 + cw],
                                      hsl(kd, h, 1), rhs,
                                      start=st, stop=False),
                             rd=[bkdR] + rdq, wr=[bBANK[pb_]])
                rdl = [bSMALL]
                if first_half:
                    rdl.append(bNMT[t])
                if second_half:
                    rdl.append(bNMT[t + 1])
                for q4 in range(4):
                    for hh in range(2):
                        h = hg8 * 8 + q4 * 2 + hh
                        pb_, slot = hb(h)
                        s.do("tensor", lambda e, h=h, slot=slot, pb_=pb_, c0=c0, cw=cw, o0=o0:
                             e.matmul(bank[pb_][:, slot + o0: slot + o0 + cw],
                                      eoh[:, h * 128:(h + 1) * 128],
                                      nmT[0:16, c0:c0 + cw],
                                      start=False, stop=False),
                             rd=rdl, wr=[bBANK[pb_]],
                             keep_pe=(q4 == 0 and hh == 0), also=([sc_last] if (q4 == 0 and hh == 0) else ()))
                if first_half and second_half:
                    mrhs = mstd[:]
                elif second_half:
                    mrhs = mj0[:]
                else:
                    mrhs = mstd[:, 0:128]
                for q4 in range(4):
                    for hh in range(2):
                        h = hg8 * 8 + q4 * 2 + hh
                        pb_, slot = hb(h)
                        s.do("tensor", lambda e, pb_=pb_, slot=slot, o0=o0, cw=cw, mrhs=mrhs, sp=(slot != 0):
                             e.matmul(bank[pb_][:, slot + o0: slot + o0 + cw],
                                      idb[:], mrhs,
                                      start=False, stop=sp),
                             rd=[bSMALL], wr=[bBANK[pb_]])
                for h in range(hg8 * 8, hg8 * 8 + 8):
                    pb_, slot = hb(h)
                    if first_half and second_half:
                        s.do("scalar", lambda e, pb_=pb_, slot=slot, h=h, ebuf=ebuf: e.activation(
                            ebuf[:, h * 256:(h + 1) * 256], bank[pb_][:, slot:slot + 256], AF.Exp),
                            rd=[bBANK[pb_]], wr=[beb])
                    else:
                        sl0, sl1 = (128, 256) if second_half else (0, 128)
                        s.do("scalar", lambda e, pb_=pb_, slot=slot, h=h, sl0=sl0, sl1=sl1, ebuf=ebuf: e.activation(
                            ebuf[:, h * 256 + sl0:h * 256 + sl1],
                            bank[pb_][:, slot + sl0:slot + sl1], AF.Exp),
                            rd=[bBANK[pb_]], wr=[beb])

        if 1 <= j - 4 <= 16:
            t = j - 4
            xT_, bxT_ = xiT[t % 2], bXIT[t % 2]
            for dg in range(8):
                for half, pb_ in ((0, 0), (1, 1)):
                    s.do("tensor", lambda e, dg=dg, half=half, pb_=pb_, xT_=xT_:
                         e.matmul(bank[pb_][:], xT_[:, dg * 128:(dg + 1) * 128],
                                  wfT[:, dg * 1024 + half * 512: dg * 1024 + (half + 1) * 512],
                                  start=(dg == 0), stop=(dg == 7)),
                         rd=[bxT_, bW["wf"][2 * half], bW["wf"][2 * half + 1]], wr=[bBANK[pb_]])
            xt, bxt = xs[t % XR], bXS[t % XR]
            s.do("vector", lambda e, xt=xt: e.tensor_tensor(xt[:], xt[:], bf_[:], op=ALU.add),
                 rd=[bSMALL, bxt], wr=[bxt])
            for half, pb_ in ((0, 0), (1, 1)):
                s.do("vector", lambda e, half=half, pb_=pb_, xt=xt: e.scalar_tensor_tensor(
                    xt[:, half * 512:(half + 1) * 512], bank[pb_][:], smax[:],
                    xt[:, half * 512:(half + 1) * 512], op0=ALU.mult, op1=ALU.add),
                    rd=[bBANK[pb_], bSC], wr=[bxt])
            s.do("scalar", lambda e, t=t, xt=xt: e.dma_start(
                out=xout[(t - 1) * 128: t * 128, :], in_=xt[:]),
                rd=[bxt], dma_sem="out")

        if 1 <= j - 2 <= 16:
            t = j - 2
            em1 = expT[(t - 1) % 2]
            em0 = expT[t % 2]
            va1, bva1 = vaug[(t - 1) % 4], bVA[(t - 1) % 4]
            va0, bva0 = vaug[t % 4], bVA[t % 4]
            groups = ((4, 0, tuple(range(0, 7))), (5, 0, tuple(range(7, 14))), (7, 0, (14, 15)))
            for pb_, off, hs in groups:
                for i, h in enumerate(hs):
                    c = off + (h - hs[0]) * 65
                    b1a, b1b = bEXPA[(t - 1) % 2], bEXPB[(t - 1) % 2]
                    b0a, b0b = bEXPA[t % 2], bEXPB[t % 2]
                    s.do("tensor", lambda e, h=h, c=c, pb_=pb_, st=(i == 0), em1=em1, va1=va1:
                         e.matmul(bank[pb_][:, c:c + 65],
                                  em1[:, h * 256 + 128: h * 256 + 256],
                                  va1[:, h * 65:(h + 1) * 65],
                                  start=st, stop=False),
                         rd=[b1a, b1b, bva1], wr=[bBANK[pb_]])
                    s.do("tensor", lambda e, h=h, c=c, pb_=pb_, sp=(i == len(hs) - 1), em0=em0, va0=va0:
                         e.matmul(bank[pb_][:, c:c + 65],
                                  em0[:, h * 256: h * 256 + 128],
                                  va0[:, h * 65:(h + 1) * 65],
                                  start=False, stop=sp),
                         rd=[b0a, b0b, bva0], wr=[bBANK[pb_]])
            for pb_, off, hs in groups:
                n = len(hs)
                s.do("scalar", lambda e, pb_=pb_, off=off, h0=hs[0], n=n: e.activation(
                    den[:, h0:h0 + n],
                    bank[pb_][:, off:off + 65 * n].rearrange("p (a b) -> p a b", a=n)[:, :, 64:65].squeeze(2),
                    AF.Identity),
                    rd=[bBANK[pb_]], wr=[bDEN])
            s.do("vector", lambda e: e.reciprocal(rec[:], den[:]), rd=[bDEN], wr=[bDEN])
            at, bat = attn[t % 2], bATTN[t % 2]
            for pb_, off, hs in groups:
                for h in hs:
                    c = off + (h - hs[0]) * 65
                    s.do("scalar", lambda e, h=h, c=c, pb_=pb_, at=at: e.activation(
                        at[:, h * 64:(h + 1) * 64], bank[pb_][:, c:c + 64], AF.Identity,
                        scale=rec[:, h:h + 1]),
                        rd=[bBANK[pb_], bDEN], wr=[bat])

        if 1 <= j - 3 <= 16:
            t = j - 3
            aT, baT = attT[t % 2], bATTT[t % 2]
            for dg in range(8):
                for half, pb_ in ((0, 1), (1, 2)):
                    s.do("tensor", lambda e, dg=dg, half=half, pb_=pb_, aT=aT:
                         e.matmul(bank[pb_][:], aT[:, dg * 128:(dg + 1) * 128],
                                  woT[:, dg * 1024 + half * 512: dg * 1024 + (half + 1) * 512],
                                  start=(dg == 0), stop=(dg == 7)),
                         rd=[baT, bW["wo"][2 * half], bW["wo"][2 * half + 1]], wr=[bBANK[pb_]])
            xt, bxt = xs[t % XR], bXS[t % XR]
            s.do("vector", lambda e, xt=xt: e.tensor_tensor(xt[:], xt[:], bo[:], op=ALU.add),
                 rd=[bSMALL, bxt], wr=[bxt])
            for half, pb_ in ((0, 1), (1, 2)):
                s.do("vector", lambda e, half=half, pb_=pb_, xt=xt: e.scalar_tensor_tensor(
                    xt[:, half * 512:(half + 1) * 512], bank[pb_][:], 1.0,
                    xt[:, half * 512:(half + 1) * 512], op0=ALU.mult, op1=ALU.add),
                    rd=[bBANK[pb_]], wr=[bxt])
            ln_stats(xt, bxt, bns2, bna2, lnt2, bLN2)
            ln_apply(xt, bxt, xn2[:], bXN2, lnt2, bLN2)
            s.do("vector", lambda e: e.tensor_scalar(xn2[:], xn2[:], rs128[:], None, op0=ALU.mult),
                 rd=[bSC], wr=[bXN2])
            s.do("vector", lambda e: e.tensor_scalar(xn2[:], xn2[:], MAGIC, MAGIC,
                                                     op0=ALU.add, op1=ALU.subtract),
                 rd=[], wr=[bXN2])
            s.do("vector", lambda e, t=t: e.tensor_scalar(xint[t % 2][:], xn2[:], -127.0, 127.0,
                                                          op0=ALU.max, op1=ALU.min),
                 rd=[bXN2], wr=[bXINT[t % 2]])
            tr_tile(xint[t % 2], bXINT[t % 2], xiT[t % 2][:], bXIT[t % 2])

        if 1 <= j - 2 <= 16:
            ta_ = j - 2
            tr_tile(attn[ta_ % 2], bATTN[ta_ % 2], attT[ta_ % 2][:], bATTT[ta_ % 2])

        if 2 <= j + 2 <= NT - 1:
            s.do("sync", lambda e, t=j + 2: e.dma_start(
                out=xs[t % XR][:], in_=xin[t * 128:(t + 1) * 128, :]),
                wr=[bXS[(j + 2) % XR]], dma_sem="ldx")

    final_out = s.last_of_dma["out"]

    with (
        nc.semaphore("pe") as sem_pe,
        nc.semaphore("dve") as sem_dve,
        nc.semaphore("act") as sem_act,
        nc.semaphore("spq") as sem_spq,
        nc.semaphore("gpq") as sem_gpq,
        nc.semaphore("lds") as sem_lds,
        nc.semaphore("ldw") as sem_ldw,
        nc.semaphore("ldx") as sem_ldx,
        nc.semaphore("wt") as sem_wt,
        nc.semaphore("outq") as sem_out,
        nc.semaphore("tr1") as sem_tr1,
        nc.semaphore("tr2") as sem_tr2,
        nc.semaphore("tr3") as sem_tr3,
        nc.Block() as block,
    ):
        sems = {"pe": sem_pe, "dve": sem_dve, "act": sem_act,
                "spq": sem_spq, "gpq": sem_gpq, "lds": sem_lds,
                "ldw": sem_ldw, "ldx": sem_ldx, "wt": sem_wt, "out": sem_out,
                "tr1": sem_tr1, "tr2": sem_tr2, "tr3": sem_tr3}
        s.ops.append(("sync", lambda e: e.nop(), {final_out}, None, False))
        s.emit(block, sems)

    for t_ in reversed(ctx):
        t_.__exit__(None, None, None)
    return nc


def _tables(h):
    if h == 0:
        pos = np.concatenate([np.zeros(128), np.arange(0, SC)])
    else:
        pos = np.arange(SC - 128, SC + SC)
    pos = pos.astype(np.float64)
    inv = 1.0 / (10000.0 ** (np.arange(32, dtype=np.float64) / 32.0))
    ang = pos[:, None] * inv[None, :]
    c = np.cos(ang).astype(np.float32)
    sn = np.sin(ang).astype(np.float32)
    c64 = np.concatenate([c, c], axis=1)
    s64 = np.concatenate([-sn, sn], axis=1)
    return c64, s64


def _eoh2():
    one = np.concatenate([np.tile((np.arange(16) == h).astype(np.float32)[:, None], (1, 128))
                          for h in range(16)], axis=1)
    e = np.zeros((80, 16 * 128), np.float32)
    e[0:16] = one
    e[64:80] = one
    return e.astype(ml_dtypes.bfloat16)


def _masks():
    p = np.arange(128)[:, None]
    c = np.arange(128)[None, :]
    blk = np.float32(-1e30)
    diag = np.where(c >= p, np.float32(0), blk)
    left = np.where(c <= p, np.float32(0), blk)
    mstd = np.concatenate([diag, left], axis=1)
    return mstd, left


def kernel(L0_attn_x, L0_attn_Wq, L0_attn_bq, L0_attn_Wk, L0_attn_bk,
           L0_attn_Wv, L0_attn_bv, L0_attn_Wo, L0_attn_bo,
           L0_ffn_ternary_weight, L0_ffn_bias, L0_ffn_act_scale):
    bfdt = ml_dtypes.bfloat16
    x = np.ascontiguousarray(np.asarray(L0_attn_x, dtype=np.float32))
    mstd, left = _masks()
    nc = build_nc()
    in_maps = []
    bcast = lambda v: np.ascontiguousarray(
        np.broadcast_to(np.asarray(v, np.float32)[None, :], (128, D)).astype(bfdt))
    common = {
        "wq": np.ascontiguousarray(np.asarray(L0_attn_Wq, np.float32)),
        "wk": np.ascontiguousarray(np.asarray(L0_attn_Wk, np.float32)),
        "wv": np.ascontiguousarray(np.asarray(L0_attn_Wv, np.float32)),
        "wo": np.ascontiguousarray(np.asarray(L0_attn_Wo, np.float32)),
        "tern": np.ascontiguousarray(np.asarray(L0_ffn_ternary_weight, np.int32)),
        "bqh": bcast(L0_attn_bq), "bkh": bcast(L0_attn_bk),
        "bvh": bcast(L0_attn_bv), "boh": bcast(L0_attn_bo),
        "bfh": bcast(L0_ffn_bias),
        "ash": np.full((128, 1), float(np.asarray(L0_ffn_act_scale)), np.float32),
        "msh": mstd.astype(bfdt),
        "idfh": np.eye(128, dtype=np.float32),
        "idbh": np.eye(128, dtype=np.float32).astype(bfdt),
        "eohh": _eoh2()[0:16],
    }
    tabs = {}
    for h in (0, 1):
        c64, s64 = _tables(h)
        tabs[h] = {"ckh": c64.astype(bfdt), "skh": s64.astype(bfdt)}
    mA1_h0 = np.concatenate([np.full((128, 128), -1e30, np.float32),
                             np.zeros((128, 128), np.float32)], axis=1)
    for core in range(8):
        b, h = core // 2, core % 2
        if h == 0:
            halo = np.zeros((128, D), np.float32)
        else:
            halo = x[b, SC - 128:SC, :]
        xin = np.ascontiguousarray(np.concatenate([halo, x[b, h * SC:(h + 1) * SC, :]], axis=0))
        m = dict(common)
        m["xin"] = xin
        m.update(tabs[h])
        m["mj0h"] = (left if h == 1 else np.full((128, 128), -1e30, np.float32)).astype(bfdt)
        m["mA1h"] = (mA1_h0 if h == 0 else np.zeros((128, 256), np.float32)).astype(bfdt)
        in_maps.append(m)
    res = run_bass_kernel_spmd(nc, in_maps, core_ids=list(range(8)))
    out = np.empty((B, S, D), np.float32)
    for core in range(8):
        b, h = core // 2, core % 2
        out[b, h * SC:(h + 1) * SC, :] = res.results[core]["xout"]
    return out



# version 28
# speedup vs baseline: 1.1874x; 1.0105x over previous
import numpy as np
import ml_dtypes
import concourse.bass as bass
import concourse.mybir as mybir
from concourse.bass_utils import run_bass_kernel_spmd

F32 = mybir.dt.float32
BF16 = mybir.dt.bfloat16
I32 = mybir.dt.int32
AF = mybir.ActivationFunctionType
ALU = mybir.AluOpType
AX = mybir.AxisListType

B, S, D = 4, 4096, 1024
H, DH = 16, 64
SC = 2048
NT = 17
MAGIC = 12582912.0
EPS = 1e-5


class Buf:
    __slots__ = ("w", "r")

    def __init__(self):
        self.w = None
        self.r = []


class Sched:
    ENGINES = ("tensor", "vector", "scalar", "sync", "gpsimd")
    DMA_SEMS = ("lds", "ldw", "ldx", "wt", "out", "tr1", "tr2", "tr3")

    def __init__(self):
        self.ops = []
        self.last_of_dma = {d: None for d in self.DMA_SEMS}
        self.own = {"tensor": "pe", "vector": "dve", "scalar": "act",
                    "sync": "spq", "gpsimd": "gpq"}

    def do(self, engine, emit, rd=(), wr=(), dma_sem=None, keep_pe=False, also=()):
        oid = len(self.ops)
        deps = set(also)
        if dma_sem is not None and self.last_of_dma[dma_sem] is not None:
            deps.add(self.last_of_dma[dma_sem])
        for b in rd:
            if b.w is not None:
                deps.add(b.w)
        for b in wr:
            if b.w is not None:
                deps.add(b.w)
            deps.update(b.r)
        self.ops.append((engine, emit, deps, dma_sem, keep_pe))
        if dma_sem is not None:
            self.last_of_dma[dma_sem] = oid
        for b in rd:
            b.r.append(oid)
        for b in wr:
            b.w = oid
            b.r = []
        return oid

    def emit(self, block, sems, final_waits=()):
        n = len(self.ops)
        engine_of = [o[0] for o in self.ops]
        dma_of = [o[3] for o in self.ops]
        keep_of = [o[4] for o in self.ops]
        sem_of = [dma_of[i] if dma_of[i] else self.own[engine_of[i]]
                  for i in range(n)]
        awaited = [False] * n
        for i, (eng, _e, deps, _d, _k) in enumerate(self.ops):
            for d in deps:
                if engine_of[i] == "tensor" and engine_of[d] == "tensor" \
                        and dma_of[d] is None and not keep_of[i]:
                    continue
                awaited[d] = True
        for d in final_waits:
            awaited[d] = True
        val = [None] * n
        cum = {}
        for i in range(n):
            if dma_of[i] is not None:
                awaited[i] = True
            if awaited[i]:
                sm = sem_of[i]
                cum[sm] = cum.get(sm, 0) + (16 if dma_of[i] else 1)
                val[i] = cum[sm]
        qs = {e: [] for e in self.ENGINES}
        last_wait = {e: {} for e in self.ENGINES}
        for i, (eng, emitfn, deps, dma, _k) in enumerate(self.ops):
            need = {}
            for d in deps:
                if engine_of[i] == "tensor" and engine_of[d] == "tensor" \
                        and dma_of[d] is None and not keep_of[i]:
                    continue
                sm, v = sem_of[d], val[d]
                assert v is not None
                if need.get(sm, 0) < v:
                    need[sm] = v
            waits = []
            lw = last_wait[eng]
            for sm, v in need.items():
                if lw.get(sm, 0) < v:
                    waits.append((sm, v))
                    lw[sm] = v
            inc = (sem_of[i], 16 if dma else 1) if awaited[i] else None
            qs[eng].append((waits, emitfn, inc))

        def mk(ops):
            def _(eng):
                for waits, emitfn, inc in ops:
                    for sm, v in waits:
                        eng.wait_ge(sems[sm], v)
                    instr = emitfn(eng)
                    if inc is not None:
                        instr.then_inc(sems[inc[0]], inc[1])
            return _
        for e in self.ENGINES:
            if qs[e]:
                getattr(block, e)(mk(qs[e]))
        return {d: (sem_of[d], val[d]) for d in final_waits}


def build_nc():
    nc = bass.Bass("TRN2", target_bir_lowering=False)

    xin = nc.dram_tensor("xin", [NT * 128, D], F32, kind="ExternalInput").ap()
    wq = nc.dram_tensor("wq", [D, D], F32, kind="ExternalInput").ap()
    wk = nc.dram_tensor("wk", [D, D], F32, kind="ExternalInput").ap()
    wv = nc.dram_tensor("wv", [D, D], F32, kind="ExternalInput").ap()
    wo = nc.dram_tensor("wo", [D, D], F32, kind="ExternalInput").ap()
    tern = nc.dram_tensor("tern", [D, D], I32, kind="ExternalInput").ap()
    bqh = nc.dram_tensor("bqh", [128, D], BF16, kind="ExternalInput").ap()
    bkh = nc.dram_tensor("bkh", [128, D], BF16, kind="ExternalInput").ap()
    bvh = nc.dram_tensor("bvh", [128, D], BF16, kind="ExternalInput").ap()
    boh = nc.dram_tensor("boh", [128, D], BF16, kind="ExternalInput").ap()
    bfh = nc.dram_tensor("bfh", [128, D], BF16, kind="ExternalInput").ap()
    ash = nc.dram_tensor("ash", [128, 1], F32, kind="ExternalInput").ap()
    ckh = nc.dram_tensor("ckh", [NT * 128, DH], BF16, kind="ExternalInput").ap()
    skh = nc.dram_tensor("skh", [NT * 128, DH], BF16, kind="ExternalInput").ap()
    msh = nc.dram_tensor("msh", [128, 256], BF16, kind="ExternalInput").ap()
    mj0h = nc.dram_tensor("mj0h", [128, 128], BF16, kind="ExternalInput").ap()
    mA1h = nc.dram_tensor("mA1h", [128, 256], BF16, kind="ExternalInput").ap()
    idfh = nc.dram_tensor("idfh", [128, 128], F32, kind="ExternalInput").ap()
    idbh = nc.dram_tensor("idbh", [128, 128], BF16, kind="ExternalInput").ap()
    eohh = nc.dram_tensor("eohh", [16, 16 * 128], BF16, kind="ExternalInput").ap()
    xout = nc.dram_tensor("xout", [SC, D], F32, kind="ExternalOutput").ap()

    s = Sched()
    ctx = []

    def sb(name, shape, dt):
        t = nc.sbuf_tensor(name, shape, dt)
        ctx.append(t)
        return t.__enter__()

    def ps(name, shape, dt):
        t = nc.psum_tensor(name, shape, dt)
        ctx.append(t)
        return t.__enter__()

    XR = 6
    xs = [sb(f"xs{i}", [128, D], F32) for i in range(XR)]
    xnst = [sb("xnst0", [128, D], BF16)] * 2
    xnT = [sb(f"xnT{i}", [128, D], BF16) for i in range(2)]
    qb = sb("qb", [128, D], BF16)
    t2 = sb("t2", [128, D], BF16)
    q_tm = [sb("qtm0", [128, D], BF16)] * 2
    k_tm = [sb("ktm0", [128, D], BF16)] * 2
    qTd = [sb(f"qTd{i}", [128, 2 * D], BF16) for i in range(3)]
    kTd = [sb(f"kTd{i}", [128, 2 * D], BF16) for i in range(3)]
    vaug = [sb(f"vaug{i}", [128, H * 65], BF16) for i in range(4)]
    expT = [sb(f"expT{i}", [128, H * 256], BF16) for i in range(2)]
    attn = [sb("attn0", [128, D], BF16)] * 2
    attT = [sb("attT0", [128, D], BF16)] * 2
    xn2 = sb("xn2", [128, D], F32)
    xint = [sb("xint0", [128, D], BF16)] * 2
    xiT = [sb("xiT0", [128, D], BF16)] * 2
    wqT = sb("wqT", [128, 8 * D], BF16)
    wkT = sb("wkT", [128, 8 * D], BF16)
    wvT = sb("wvT", [128, 8 * D], BF16)
    woT = sb("woT", [128, 8 * D], BF16)
    wfT = sb("wfT", [128, 8 * D], BF16)
    bq = sb("bq", [128, D], BF16)
    bk = sb("bk", [128, D], BF16)
    bv = sb("bv", [128, D], BF16)
    bo = sb("bo", [128, D], BF16)
    bf_ = sb("bf_", [128, D], BF16)
    as128 = sb("as128", [128, 1], F32)
    smax = sb("smax", [128, 1], F32)
    rs128 = sb("rs128", [128, 1], F32)
    ck = sb("ck", [128, NT * DH], BF16)
    sk_ = sb("sk_", [128, NT * DH], BF16)
    mstd = sb("mstd", [128, 256], BF16)
    mj0 = sb("mj0", [128, 128], BF16)
    mA1 = sb("mA1", [128, 256], BF16)
    idf = sb("idf", [128, 128], F32)
    idb = sb("idb", [128, 128], BF16)
    eoh = sb("eoh", [16, 16 * 128], BF16)
    nmT = sb("nmT", [16, NT * 128], BF16)
    nmt = sb("nmt", [128, 16], F32)
    bna = sb("bna", [128, 2], F32)
    bns = sb("bns", [128, 12], F32)
    lnt = sb("lnt", [128, 4], F32)
    bna2 = sb("bna2", [128, 2], F32)
    bns2 = sb("bns2", [128, 12], F32)
    lnt2 = sb("lnt2", [128, 4], F32)
    den = sb("den", [128, 16], F32)
    rec = sb("rec", [128, 16], F32)
    cst = sb("cst", [128, 2], F32)
    wstg = sb("wstg", [128, 2 * D], BF16)
    nc.const_aps.register(F32, EPS, cst[:, 0:1])
    nc.const_aps.register(F32, -0.5, cst[:, 1:2])

    bank = [ps(f"pb{i}", [128, 512], F32) for i in range(8)]

    bXS = [Buf() for _ in range(XR)]
    _bxn = Buf(); bXNS = [_bxn, _bxn]
    bXNT = [Buf() for _ in range(2)]
    bQB, bT2 = Buf(), Buf()
    _bq = Buf(); _bk = Buf()
    bQTM = [_bq, _bq]
    bKTM = [_bk, _bk]
    bQTdL = [Buf() for _ in range(3)]
    bQTdR = [Buf() for _ in range(3)]
    bKTdL = [Buf() for _ in range(3)]
    bKTdR = [Buf() for _ in range(3)]
    bVA = [Buf() for _ in range(4)]
    bEXPA = [Buf() for _ in range(2)]
    bEXPB = [Buf() for _ in range(2)]
    _ba = Buf(); bATTN = [_ba, _ba]
    _bat = Buf(); bATTT = [_bat, _bat]
    bXN2 = Buf()
    _bxi = Buf(); bXINT = [_bxi, _bxi]
    _bxt2 = Buf(); bXIT = [_bxt2, _bxt2]
    bW = {n: [Buf() for _ in range(4)] for n in ("wq", "wk", "wv", "wo", "wf")}
    bWSTG = Buf()
    bSMALL = Buf()
    bNMT = [Buf() for _ in range(NT)]
    bNM = Buf()
    bLN = Buf()
    bLN2 = Buf()
    bDEN = Buf()
    bCONST = Buf()
    bBANK = [Buf() for _ in range(8)]
    bSC = Buf()

    wT = {"wq": wqT, "wk": wkT, "wv": wvT, "wo": woT, "wf": wfT}

    trn = [0]

    def dma_T(dst_ap, src_ap, rd, wr):
        sem = ("tr1", "tr1", "tr1")[trn[0] % 3]
        eng = "scalar" if sem == "tr3" else "sync"
        trn[0] += 1
        s.do(eng, lambda e, d=dst_ap, sr=src_ap: e.dma_start_transpose(d, sr),
             rd=rd, wr=wr, dma_sem=sem)


    s.do("vector", lambda e: e.memset(cst[:, 0:1], EPS), wr=[bCONST])
    s.do("vector", lambda e: e.memset(cst[:, 1:2], -0.5), wr=[bCONST])
    s.do("vector", lambda e: e.memset(nmT[:], 0.0), wr=[bNMT[0]])
    for i in range(4):
        s.do("vector", lambda e, i=i: e.memset(
            vaug[i][:].rearrange("p (h c) -> p h c", h=H)[:, :, 64:65], 1.0),
            wr=[bVA[i]])

    qctr = [0]

    def prep_quarter(name, src, hf):
        dst = wT[name]
        qi = qctr[0]
        qctr[0] += 1
        if qi % 2 == 0:
            stg = [wstg[:, 0:1024], wstg[:, 1024:2048]]
            bstg = [bWSTG, bWSTG]
        else:
            stg = [expT[1][:, 2048:3072], expT[1][:, 3072:4096]]
            bstg = [bEXPB[1], bEXPB[1]]
        for g in range(2):
            s.do("scalar", lambda e, hf=hf, g=g, src=src: e.dma_start(
                out=xs[4 + g][:],
                in_=src[hf * 256 + g * 128: hf * 256 + (g + 1) * 128, :]),
                wr=[bXS[4 + g]], dma_sem=("wt", "ldw")[(2 * qi + g) % 2])
        for g in range(2):
            if name == "wq":
                s.do("vector", lambda e, g=g, stg=stg: e.tensor_scalar_mul(
                    stg[g], xs[4 + g][:], 0.125),
                    rd=[bXS[4 + g]], wr=[bstg[g]])
            else:
                s.do("vector", lambda e, g=g, stg=stg: e.tensor_copy(
                    stg[g], xs[4 + g][:]),
                    rd=[bXS[4 + g]], wr=[bstg[g]])
        for og4 in range(2):
            og = hf * 2 + og4
            s.do("sync", lambda e, og=og, og4=og4, dst=dst, stg=stg: e.dma_start_transpose(
                dst[:].rearrange("p (g f) -> p g f", g=8)[:, :, og * 128:(og + 1) * 128],
                stg[og4].rearrange("p (g f) -> p g f", g=8)),
                rd=[bstg[og4]], wr=[bW[name][hf]], dma_sem="tr1")

    s.do("sync", lambda e: e.dma_start(out=xs[0][:], in_=xin[0:128, :]),
         wr=[bXS[0]], dma_sem="ldx")
    s.do("sync", lambda e: e.dma_start(out=xs[1][:], in_=xin[128:256, :]),
         wr=[bXS[1]], dma_sem="ldx")
    def ln_stats(xbuf, bx, bnsX, bnaX, lntX, bL):
        s.do("vector", lambda e: e.bn_stats(bnsX[:, 0:6], xbuf[:, 0:512]), rd=[bx], wr=[bL])
        s.do("vector", lambda e: e.bn_stats(bnsX[:, 6:12], xbuf[:, 512:1024]), rd=[bx], wr=[bL])
        s.do("vector", lambda e: e.bn_aggr(bnaX[:], bnsX[:]), rd=[bL], wr=[bL])
        s.do("scalar", lambda e: e.activation(lntX[:, 2:3], bnaX[:, 1:2], AF.Ln, bias=EPS),
             rd=[bL, bCONST], wr=[bL])
        s.do("scalar", lambda e: e.activation(lntX[:, 0:1], lntX[:, 2:3], AF.Exp, scale=-0.5),
             rd=[bL, bCONST], wr=[bL])
        s.do("vector", lambda e: e.tensor_scalar(lntX[:, 1:2], bnaX[:, 0:1], lntX[:, 0:1],
                                                 -1.0, op0=ALU.mult, op1=ALU.mult),
             rd=[bL], wr=[bL])

    def ln_apply(xbuf, bx, out, bout, lntX, bL):
        s.do("scalar", lambda e: e.activation(out, xbuf[:], AF.Identity,
                                              bias=lntX[:, 1:2], scale=lntX[:, 0:1]),
             rd=[bx, bL], wr=[bout])

    def rope(proj_banks, bias_sb, j, outbuf, bout):
        pb0, pb1 = proj_banks
        s.do("vector", lambda e: e.scalar_tensor_tensor(
            qb[:, 0:512], bank[pb0][:], 1.0, bias_sb[:, 0:512], op0=ALU.mult, op1=ALU.add),
            rd=[bBANK[pb0], bSMALL], wr=[bQB])
        s.do("vector", lambda e: e.scalar_tensor_tensor(
            qb[:, 512:1024], bank[pb1][:], 1.0, bias_sb[:, 512:1024], op0=ALU.mult, op1=ALU.add),
            rd=[bBANK[pb1], bSMALL], wr=[bQB])
        qb3 = qb[:].rearrange("p (h t f) -> p h t f", h=H, t=2)
        c3 = ck[:, j * 64:(j + 1) * 64].rearrange("p (t f) -> p t f", t=2) \
            .unsqueeze(1).broadcast_to([128, H, 2, 32])
        s3 = sk_[:, j * 64:(j + 1) * 64].rearrange("p (t f) -> p t f", t=2) \
            .unsqueeze(1).broadcast_to([128, H, 2, 32])
        o3 = outbuf[:].rearrange("p (h t f) -> p h t f", h=H, t=2)
        t23 = t2[:].rearrange("p (h t f) -> p h t f", h=H, t=2)
        s.do("vector", lambda e: e.tensor_tensor(t23[:, :, 0:1, :], qb3[:, :, 1:2, :],
                                                 s3[:, :, 0:1, :], op=ALU.mult),
             rd=[bQB, bSMALL], wr=[bT2])
        s.do("vector", lambda e: e.tensor_tensor(t23[:, :, 1:2, :], qb3[:, :, 0:1, :],
                                                 s3[:, :, 1:2, :], op=ALU.mult),
             rd=[bQB, bSMALL], wr=[bT2])
        s.do("vector", lambda e: e.tensor_tensor(o3, qb3, c3, op=ALU.mult),
             rd=[bQB, bSMALL], wr=[bout])
        s.do("vector", lambda e: e.tensor_tensor(outbuf[:], outbuf[:], t2[:], op=ALU.add),
             rd=[bT2], wr=[bout])

    def tr_tile(src, bsrc, dst_ap, bw):
        dma_T(dst_ap.rearrange("p (g f) -> p g f", g=8),
              src[:].rearrange("p (g f) -> p g f", g=8), rd=[bsrc], wr=[bw])

    def hsl(t_, h, half=None, width=128):
        p0 = (h % 2) * 64
        c0 = (h // 2) * width + (0 if half is None else half * 1024)
        return t_[p0:p0 + 64, c0:c0 + width]

    def ln1(t):
        xsl, bx = xs[t % XR], bXS[t % XR]
        ln_stats(xsl, bx, bns, bna, lnt, bLN)
        ln_apply(xsl, bx, xnst[t % 2][:], bXNS[t % 2], lnt, bLN)
        tr_tile(xnst[t % 2], bXNS[t % 2], xnT[t % 2][:], bXNT[t % 2])

    ln1(0)
    _ssem = ("lds", "out", "tr3")
    for si, (dst, src) in enumerate(((bq, bqh), (bk, bkh), (bv, bvh), (bo, boh), (bf_, bfh),
                                     (mstd, msh), (mj0, mj0h), (mA1, mA1h),
                                     (idf, idfh), (idb, idbh), (eoh, eohh), (as128, ash))):
        s.do("scalar", lambda e, d=dst, sr=src: e.dma_start(out=d[:], in_=sr),
             wr=[bSMALL], dma_sem=_ssem[si % 3])
    for si, (dst, src) in enumerate(((ck, ckh), (sk_, skh))):
        s.do("scalar", lambda e, d=dst, sr=src: e.dma_start(
            out=d[:].rearrange("p (t f) -> p t f", t=NT),
            in_=sr.rearrange("(t p) f -> p t f", p=128)),
            wr=[bSMALL], dma_sem=_ssem[si % 3])
    s.do("vector", lambda e: e.tensor_scalar_max(smax[:], as128[:], 1e-5),
         rd=[bSMALL], wr=[bSC])
    s.do("vector", lambda e: e.reciprocal(rs128[:], smax[:]), rd=[bSC], wr=[bSC])
    s.do("vector", lambda e: e.tensor_scalar_mul(bq[:], bq[:], 0.125),
         rd=[bSMALL], wr=[bSMALL])
    for hf in range(4):
        prep_quarter("wq", wq, hf)
        prep_quarter("wk", wk, hf)
        prep_quarter("wv", wv, hf)
    for hf in range(4):
        prep_quarter("wo", wo, hf)
    wstgF = expT[1]
    bSTF = [bEXPA[1], bEXPB[1]]
    for hf in range(4):
        s.do("gpsimd", lambda e, hf=hf: e.dma_start(
            out=wstgF[:, 0:2048].rearrange("p (g f) -> p g f", g=2),
            in_=tern[hf * 256:(hf + 1) * 256, :].rearrange("(g p) f -> p g f", p=128)),
            rd=bSTF, wr=bSTF, dma_sem="tr2")
        s.do("vector", lambda e: e.tensor_scalar_add(wstgF[:, 0:2048], wstgF[:, 0:2048], -1.0),
             rd=bSTF, wr=bSTF)
        for og4 in range(2):
            og = hf * 2 + og4
            s.do("sync", lambda e, og=og, og4=og4: e.dma_start_transpose(
                wT["wf"][:].rearrange("p (g f) -> p g f", g=8)[:, :, og * 128:(og + 1) * 128],
                wstgF[:, og4 * 1024:(og4 + 1) * 1024].rearrange("p (g f) -> p g f", g=8)),
                rd=bSTF, wr=[bW["wf"][hf]], dma_sem="tr1")


    for j in range(0, NT + 4):

        t = j - 1
        doA = 1 <= t <= 16
        doQ = j <= 16
        if doA:
            qd, bqdL = qTd[t % 3], bQTdL[t % 3]
            kd = kTd[t % 3]
            bkdL, bkdR = bKTdL[t % 3], bKTdR[t % 3]

            def passA_group(hg, t=t, qd=qd, kd=kd, bqdL=bqdL, bkdL=bkdL, bkdR=bkdR):
                slot = (hg % 2) * 256
                for hh in range(2):
                    h = hg * 2 + hh
                    pb_ = 6 + hh
                    rhs = kd[(h % 2) * 64:(h % 2) * 64 + 64, :] \
                        .rearrange("p (a c) -> p a c", a=2)[:, :, (h // 2) * 128:(h // 2) * 128 + 128]
                    s.do("tensor", lambda e, h=h, slot=slot, pb_=pb_, qd=qd, rhs=rhs:
                         e.matmul(bank[pb_][:, slot:slot + 256],
                                  hsl(qd, h, 0), rhs,
                                  start=True, stop=True),
                         rd=[bqdL, bkdL, bkdR], wr=[bBANK[pb_]])
                for hh in range(2):
                    h = hg * 2 + hh
                    pb_ = 6 + hh
                    if t == 1:
                        s.do("vector", lambda e, pb_=pb_, slot=slot: e.tensor_tensor(
                            bank[pb_][:, slot:slot + 256], bank[pb_][:, slot:slot + 256],
                            mA1[:], op=ALU.add),
                            rd=[bSMALL], wr=[bBANK[pb_]])
                    s.do("vector", lambda e, h=h, pb_=pb_, slot=slot: e.reduce_max(
                        nmt[:, h:h + 1], bank[pb_][:, slot:slot + 256],
                        axis=AX.X, negate=True),
                        rd=[bBANK[pb_]], wr=[bNM])
        if doQ:
            xnsl, bxnsl = xnT[j % 2], bXNT[j % 2]

            def proj_mms(wt_, bw_, pb0, pb1, inter=None):
                for dg in range(8):
                    lhs = xnsl[:, dg * 128:(dg + 1) * 128]
                    for half, pb_ in ((0, pb0), (1, pb1)):
                        s.do("tensor", lambda e, lhs=lhs, wt_=wt_, dg=dg, half=half, pb_=pb_, st=(dg == 0), sp=(dg == 7):
                             e.matmul(bank[pb_][:], lhs,
                                      wt_[:, dg * 1024 + half * 512: dg * 1024 + (half + 1) * 512],
                                      start=st, stop=sp),
                             rd=[bxnsl, bw_[2 * half], bw_[2 * half + 1]], wr=[bBANK[pb_]])
                    if inter is not None and dg % 2 == 1:
                        passA_group(inter + dg // 2)

        if doQ:
            proj_mms(wvT, bW["wv"], 4, 5, inter=(0 if doA else None))
        elif doA:
            for hg in range(4):
                passA_group(hg)
        if doQ:
            va, bva = vaug[j % 4], bVA[j % 4]
            va3 = va[:].rearrange("p (h c) -> p h c", h=H)[:, :, 0:64]
            bv3 = bv[:].rearrange("p (h c) -> p h c", h=H)
            for half, pb_ in ((0, 4), (1, 5)):
                s.do("vector", lambda e, half=half, pb_=pb_, va3=va3, bv3=bv3: e.scalar_tensor_tensor(
                    va3[:, half * 8:(half + 1) * 8, :],
                    bank[pb_][:].rearrange("p (h c) -> p h c", h=8),
                    1.0, bv3[:, half * 8:(half + 1) * 8, :],
                    op0=ALU.mult, op1=ALU.add),
                    rd=[bBANK[pb_], bSMALL], wr=[bva])
        if doQ:
            proj_mms(wqT, bW["wq"], 0, 1, inter=(4 if doA else None))
        elif doA:
            for hg in range(4):
                passA_group(4 + hg)

        if 1 <= j - 1 <= 16:
            t = j - 1
            s.do("tensor", lambda e: e.transpose(bank[6][0:16, 0:128], nmt[:], idf[:]),
                 rd=[bNM, bSMALL], wr=[bBANK[6]])
            s.do("vector", lambda e, t=t: e.tensor_copy(nmT[0:16, t * 128:(t + 1) * 128],
                                                        bank[6][0:16, 0:128]),
                 rd=[bBANK[6]], wr=[bNMT[t]])

        if doQ and j >= 1:
            rope((0, 1), bq, j, q_tm[j % 2], bQTM[j % 2])
            tr_tile(q_tm[j % 2], bQTM[j % 2], qTd[j % 3][:][:, 0:1024], bQTdL[j % 3])
            tr_tile(q_tm[j % 2], bQTM[j % 2], qTd[(j - 1) % 3][:][:, 1024:2048], bQTdR[(j - 1) % 3])

        if doQ:
            proj_mms(wkT, bW["wk"], 2, 3)
            rope((2, 3), bk, j, k_tm[j % 2], bKTM[j % 2])
            tr_tile(k_tm[j % 2], bKTM[j % 2], kTd[j % 3][:][:, 1024:2048], bKTdR[j % 3])
            if j + 1 <= 16:
                tr_tile(k_tm[j % 2], bKTM[j % 2], kTd[(j + 1) % 3][:][:, 0:1024], bKTdL[(j + 1) % 3])

        if j + 1 <= 16:
            ln1(j + 1)

        EBP = {0: 0, 2: 1, 1: 2, 3: 3, 4: 4, 6: 5, 5: 6, 7: 7}
        eb = lambda h: (h // 8) * 8 + EBP[h % 8]

        if 0 <= j - 2 <= 16:
            t = j - 2
            ebuf = expT[t % 2]
            kd = kTd[t % 3]
            bkdR = bKTdR[t % 3]
            qd, bqdL, bqdR = qTd[t % 3], bQTdL[t % 3], bQTdR[t % 3]
            first_half = t >= 1
            second_half = t <= 15
            for hg8 in (1, 0):
                pbs = (4, 5, 6, 7) if hg8 == 0 else (0, 1, 2, 3)
                beb = (bEXPA if hg8 == 0 else bEXPB)[t % 2]
                def hb(h):
                    q = (h % 8) // 2
                    return pbs[2 * (q // 2) + (h % 2)], (q % 2) * 256
                for q4 in range(4):
                    for hh in range(2):
                        h = hg8 * 8 + q4 * 2 + hh
                        pb_, slot = hb(h)
                        if first_half and second_half:
                            rhs = qd[(h % 2) * 64:(h % 2) * 64 + 64, :] \
                                .rearrange("p (a c) -> p a c", a=2)[:, :, (h // 2) * 128:(h // 2) * 128 + 128]
                            o0, cw, c0 = 0, 256, t * 128
                            rdq = [bqdL, bqdR]
                        elif second_half:
                            rhs = hsl(qd, h, 1)
                            o0, cw, c0 = 128, 128, (t + 1) * 128
                            rdq = [bqdR]
                        else:
                            rhs = hsl(qd, h, 0)
                            o0, cw, c0 = 0, 128, t * 128
                            rdq = [bqdL]
                        sc_last = s.do("tensor", lambda e, h=h, slot=slot, pb_=pb_, kd=kd, rhs=rhs, o0=o0, cw=cw, st=(slot == 0):
                             e.matmul(bank[pb_][:, slot + o0: slot + o0 + cw],
                                      hsl(kd, h, 1), rhs,
                                      start=st, stop=False),
                             rd=[bkdR] + rdq, wr=[bBANK[pb_]])
                rdl = [bSMALL]
                if first_half:
                    rdl.append(bNMT[t])
                if second_half:
                    rdl.append(bNMT[t + 1])
                for q4 in range(4):
                    for hh in range(2):
                        h = hg8 * 8 + q4 * 2 + hh
                        pb_, slot = hb(h)
                        s.do("tensor", lambda e, h=h, slot=slot, pb_=pb_, c0=c0, cw=cw, o0=o0:
                             e.matmul(bank[pb_][:, slot + o0: slot + o0 + cw],
                                      eoh[:, h * 128:(h + 1) * 128],
                                      nmT[0:16, c0:c0 + cw],
                                      start=False, stop=False),
                             rd=rdl, wr=[bBANK[pb_]],
                             keep_pe=(q4 == 0 and hh == 0), also=([sc_last] if (q4 == 0 and hh == 0) else ()))
                if first_half and second_half:
                    mrhs = mstd[:]
                elif second_half:
                    mrhs = mj0[:]
                else:
                    mrhs = mstd[:, 0:128]
                for q4 in range(4):
                    for hh in range(2):
                        h = hg8 * 8 + q4 * 2 + hh
                        pb_, slot = hb(h)
                        s.do("tensor", lambda e, pb_=pb_, slot=slot, o0=o0, cw=cw, mrhs=mrhs, sp=(slot != 0):
                             e.matmul(bank[pb_][:, slot + o0: slot + o0 + cw],
                                      idb[:], mrhs,
                                      start=False, stop=sp),
                             rd=[bSMALL], wr=[bBANK[pb_]])
                if first_half and second_half:
                    for q2 in range(4):
                        hA = hg8 * 8 + (0, 1, 4, 5)[q2]
                        pb_, _ = hb(hA)
                        e0 = eb(hA) * 256
                        s.do("scalar", lambda e, pb_=pb_, e0=e0, ebuf=ebuf: e.activation(
                            ebuf[:, e0:e0 + 512], bank[pb_][:, 0:512], AF.Exp),
                            rd=[bBANK[pb_]], wr=[beb])
                else:
                    sl0, sl1 = (128, 256) if second_half else (0, 128)
                    for h in range(hg8 * 8, hg8 * 8 + 8):
                        pb_, slot = hb(h)
                        s.do("scalar", lambda e, pb_=pb_, slot=slot, h=h, sl0=sl0, sl1=sl1, ebuf=ebuf: e.activation(
                            ebuf[:, eb(h) * 256 + sl0:eb(h) * 256 + sl1],
                            bank[pb_][:, slot + sl0:slot + sl1], AF.Exp),
                            rd=[bBANK[pb_]], wr=[beb])

        if 1 <= j - 4 <= 16:
            t = j - 4
            xT_, bxT_ = xiT[t % 2], bXIT[t % 2]
            for dg in range(8):
                for half, pb_ in ((0, 0), (1, 1)):
                    s.do("tensor", lambda e, dg=dg, half=half, pb_=pb_, xT_=xT_:
                         e.matmul(bank[pb_][:], xT_[:, dg * 128:(dg + 1) * 128],
                                  wfT[:, dg * 1024 + half * 512: dg * 1024 + (half + 1) * 512],
                                  start=(dg == 0), stop=(dg == 7)),
                         rd=[bxT_, bW["wf"][2 * half], bW["wf"][2 * half + 1]], wr=[bBANK[pb_]])
            xt, bxt = xs[t % XR], bXS[t % XR]
            s.do("vector", lambda e, xt=xt: e.tensor_tensor(xt[:], xt[:], bf_[:], op=ALU.add),
                 rd=[bSMALL, bxt], wr=[bxt])
            for half, pb_ in ((0, 0), (1, 1)):
                s.do("vector", lambda e, half=half, pb_=pb_, xt=xt: e.scalar_tensor_tensor(
                    xt[:, half * 512:(half + 1) * 512], bank[pb_][:], smax[:],
                    xt[:, half * 512:(half + 1) * 512], op0=ALU.mult, op1=ALU.add),
                    rd=[bBANK[pb_], bSC], wr=[bxt])
            s.do("scalar", lambda e, t=t, xt=xt: e.dma_start(
                out=xout[(t - 1) * 128: t * 128, :], in_=xt[:]),
                rd=[bxt], dma_sem="out")

        if 1 <= j - 2 <= 16:
            t = j - 2
            em1 = expT[(t - 1) % 2]
            em0 = expT[t % 2]
            va1, bva1 = vaug[(t - 1) % 4], bVA[(t - 1) % 4]
            va0, bva0 = vaug[t % 4], bVA[t % 4]
            groups = ((4, 0, tuple(range(0, 7))), (5, 0, tuple(range(7, 14))), (7, 0, (14, 15)))
            for pb_, off, hs in groups:
                for i, h in enumerate(hs):
                    c = off + (h - hs[0]) * 65
                    b1a, b1b = bEXPA[(t - 1) % 2], bEXPB[(t - 1) % 2]
                    b0a, b0b = bEXPA[t % 2], bEXPB[t % 2]
                    s.do("tensor", lambda e, h=h, c=c, pb_=pb_, st=(i == 0), em1=em1, va1=va1:
                         e.matmul(bank[pb_][:, c:c + 65],
                                  em1[:, eb(h) * 256 + 128: eb(h) * 256 + 256],
                                  va1[:, h * 65:(h + 1) * 65],
                                  start=st, stop=False),
                         rd=[b1a, b1b, bva1], wr=[bBANK[pb_]])
                    s.do("tensor", lambda e, h=h, c=c, pb_=pb_, sp=(i == len(hs) - 1), em0=em0, va0=va0:
                         e.matmul(bank[pb_][:, c:c + 65],
                                  em0[:, eb(h) * 256: eb(h) * 256 + 128],
                                  va0[:, h * 65:(h + 1) * 65],
                                  start=False, stop=sp),
                         rd=[b0a, b0b, bva0], wr=[bBANK[pb_]])
            for pb_, off, hs in groups:
                n = len(hs)
                s.do("scalar", lambda e, pb_=pb_, off=off, h0=hs[0], n=n: e.activation(
                    den[:, h0:h0 + n],
                    bank[pb_][:, off:off + 65 * n].rearrange("p (a b) -> p a b", a=n)[:, :, 64:65].squeeze(2),
                    AF.Identity),
                    rd=[bBANK[pb_]], wr=[bDEN])
            s.do("vector", lambda e: e.reciprocal(rec[:], den[:]), rd=[bDEN], wr=[bDEN])
            at, bat = attn[t % 2], bATTN[t % 2]
            for pb_, off, hs in groups:
                for h in hs:
                    c = off + (h - hs[0]) * 65
                    s.do("scalar", lambda e, h=h, c=c, pb_=pb_, at=at: e.activation(
                        at[:, h * 64:(h + 1) * 64], bank[pb_][:, c:c + 64], AF.Identity,
                        scale=rec[:, h:h + 1]),
                        rd=[bBANK[pb_], bDEN], wr=[bat])

        if 1 <= j - 3 <= 16:
            t = j - 3
            aT, baT = attT[t % 2], bATTT[t % 2]
            for dg in range(8):
                for half, pb_ in ((0, 1), (1, 2)):
                    s.do("tensor", lambda e, dg=dg, half=half, pb_=pb_, aT=aT:
                         e.matmul(bank[pb_][:], aT[:, dg * 128:(dg + 1) * 128],
                                  woT[:, dg * 1024 + half * 512: dg * 1024 + (half + 1) * 512],
                                  start=(dg == 0), stop=(dg == 7)),
                         rd=[baT, bW["wo"][2 * half], bW["wo"][2 * half + 1]], wr=[bBANK[pb_]])
            xt, bxt = xs[t % XR], bXS[t % XR]
            s.do("vector", lambda e, xt=xt: e.tensor_tensor(xt[:], xt[:], bo[:], op=ALU.add),
                 rd=[bSMALL, bxt], wr=[bxt])
            for half, pb_ in ((0, 1), (1, 2)):
                s.do("vector", lambda e, half=half, pb_=pb_, xt=xt: e.scalar_tensor_tensor(
                    xt[:, half * 512:(half + 1) * 512], bank[pb_][:], 1.0,
                    xt[:, half * 512:(half + 1) * 512], op0=ALU.mult, op1=ALU.add),
                    rd=[bBANK[pb_]], wr=[bxt])
            ln_stats(xt, bxt, bns2, bna2, lnt2, bLN2)
            ln_apply(xt, bxt, xn2[:], bXN2, lnt2, bLN2)
            s.do("vector", lambda e: e.tensor_scalar(xn2[:], xn2[:], rs128[:], None, op0=ALU.mult),
                 rd=[bSC], wr=[bXN2])
            s.do("vector", lambda e: e.tensor_scalar(xn2[:], xn2[:], MAGIC, MAGIC,
                                                     op0=ALU.add, op1=ALU.subtract),
                 rd=[], wr=[bXN2])
            s.do("vector", lambda e, t=t: e.tensor_scalar(xint[t % 2][:], xn2[:], -127.0, 127.0,
                                                          op0=ALU.max, op1=ALU.min),
                 rd=[bXN2], wr=[bXINT[t % 2]])
            tr_tile(xint[t % 2], bXINT[t % 2], xiT[t % 2][:], bXIT[t % 2])

        if 1 <= j - 2 <= 16:
            ta_ = j - 2
            tr_tile(attn[ta_ % 2], bATTN[ta_ % 2], attT[ta_ % 2][:], bATTT[ta_ % 2])

        if 2 <= j + 2 <= NT - 1:
            s.do("sync", lambda e, t=j + 2: e.dma_start(
                out=xs[t % XR][:], in_=xin[t * 128:(t + 1) * 128, :]),
                wr=[bXS[(j + 2) % XR]], dma_sem="ldx")

    final_out = s.last_of_dma["out"]

    with (
        nc.semaphore("pe") as sem_pe,
        nc.semaphore("dve") as sem_dve,
        nc.semaphore("act") as sem_act,
        nc.semaphore("spq") as sem_spq,
        nc.semaphore("gpq") as sem_gpq,
        nc.semaphore("lds") as sem_lds,
        nc.semaphore("ldw") as sem_ldw,
        nc.semaphore("ldx") as sem_ldx,
        nc.semaphore("wt") as sem_wt,
        nc.semaphore("outq") as sem_out,
        nc.semaphore("tr1") as sem_tr1,
        nc.semaphore("tr2") as sem_tr2,
        nc.semaphore("tr3") as sem_tr3,
        nc.Block() as block,
    ):
        sems = {"pe": sem_pe, "dve": sem_dve, "act": sem_act,
                "spq": sem_spq, "gpq": sem_gpq, "lds": sem_lds,
                "ldw": sem_ldw, "ldx": sem_ldx, "wt": sem_wt, "out": sem_out,
                "tr1": sem_tr1, "tr2": sem_tr2, "tr3": sem_tr3}
        s.ops.append(("sync", lambda e: e.nop(), {final_out}, None, False))
        s.emit(block, sems)

    for t_ in reversed(ctx):
        t_.__exit__(None, None, None)
    return nc


def _tables(h):
    if h == 0:
        pos = np.concatenate([np.zeros(128), np.arange(0, SC)])
    else:
        pos = np.arange(SC - 128, SC + SC)
    pos = pos.astype(np.float64)
    inv = 1.0 / (10000.0 ** (np.arange(32, dtype=np.float64) / 32.0))
    ang = pos[:, None] * inv[None, :]
    c = np.cos(ang).astype(np.float32)
    sn = np.sin(ang).astype(np.float32)
    c64 = np.concatenate([c, c], axis=1)
    s64 = np.concatenate([-sn, sn], axis=1)
    return c64, s64


def _eoh2():
    one = np.concatenate([np.tile((np.arange(16) == h).astype(np.float32)[:, None], (1, 128))
                          for h in range(16)], axis=1)
    e = np.zeros((80, 16 * 128), np.float32)
    e[0:16] = one
    e[64:80] = one
    return e.astype(ml_dtypes.bfloat16)


def _masks():
    p = np.arange(128)[:, None]
    c = np.arange(128)[None, :]
    blk = np.float32(-1e30)
    diag = np.where(c >= p, np.float32(0), blk)
    left = np.where(c <= p, np.float32(0), blk)
    mstd = np.concatenate([diag, left], axis=1)
    return mstd, left


def kernel(L0_attn_x, L0_attn_Wq, L0_attn_bq, L0_attn_Wk, L0_attn_bk,
           L0_attn_Wv, L0_attn_bv, L0_attn_Wo, L0_attn_bo,
           L0_ffn_ternary_weight, L0_ffn_bias, L0_ffn_act_scale):
    bfdt = ml_dtypes.bfloat16
    x = np.ascontiguousarray(np.asarray(L0_attn_x, dtype=np.float32))
    mstd, left = _masks()
    nc = build_nc()
    in_maps = []
    bcast = lambda v: np.ascontiguousarray(
        np.broadcast_to(np.asarray(v, np.float32)[None, :], (128, D)).astype(bfdt))
    common = {
        "wq": np.ascontiguousarray(np.asarray(L0_attn_Wq, np.float32)),
        "wk": np.ascontiguousarray(np.asarray(L0_attn_Wk, np.float32)),
        "wv": np.ascontiguousarray(np.asarray(L0_attn_Wv, np.float32)),
        "wo": np.ascontiguousarray(np.asarray(L0_attn_Wo, np.float32)),
        "tern": np.ascontiguousarray(np.asarray(L0_ffn_ternary_weight, np.int32)),
        "bqh": bcast(L0_attn_bq), "bkh": bcast(L0_attn_bk),
        "bvh": bcast(L0_attn_bv), "boh": bcast(L0_attn_bo),
        "bfh": bcast(L0_ffn_bias),
        "ash": np.full((128, 1), float(np.asarray(L0_ffn_act_scale)), np.float32),
        "msh": mstd.astype(bfdt),
        "idfh": np.eye(128, dtype=np.float32),
        "idbh": np.eye(128, dtype=np.float32).astype(bfdt),
        "eohh": _eoh2()[0:16],
    }
    tabs = {}
    for h in (0, 1):
        c64, s64 = _tables(h)
        tabs[h] = {"ckh": c64.astype(bfdt), "skh": s64.astype(bfdt)}
    mA1_h0 = np.concatenate([np.full((128, 128), -1e30, np.float32),
                             np.zeros((128, 128), np.float32)], axis=1)
    for core in range(8):
        b, h = core // 2, core % 2
        if h == 0:
            halo = np.zeros((128, D), np.float32)
        else:
            halo = x[b, SC - 128:SC, :]
        xin = np.ascontiguousarray(np.concatenate([halo, x[b, h * SC:(h + 1) * SC, :]], axis=0))
        m = dict(common)
        m["xin"] = xin
        m.update(tabs[h])
        m["mj0h"] = (left if h == 1 else np.full((128, 128), -1e30, np.float32)).astype(bfdt)
        m["mA1h"] = (mA1_h0 if h == 0 else np.zeros((128, 256), np.float32)).astype(bfdt)
        in_maps.append(m)
    res = run_bass_kernel_spmd(nc, in_maps, core_ids=list(range(8)))
    out = np.empty((B, S, D), np.float32)
    for core in range(8):
        b, h = core // 2, core % 2
        out[b, h * SC:(h + 1) * SC, :] = res.results[core]["xout"]
    return out



# version 29 (final)
# speedup vs baseline: 1.2305x; 1.0363x over previous
import numpy as np
import ml_dtypes
import concourse.bass as bass
import concourse.mybir as mybir
from concourse.bass_utils import run_bass_kernel_spmd

F32 = mybir.dt.float32
BF16 = mybir.dt.bfloat16
I32 = mybir.dt.int32
AF = mybir.ActivationFunctionType
ALU = mybir.AluOpType
AX = mybir.AxisListType

B, S, D = 4, 4096, 1024
H, DH = 16, 64
SC = 2048
NT = 17
MAGIC = 12582912.0
EPS = 1e-5


class Buf:
    __slots__ = ("w", "r")

    def __init__(self):
        self.w = None
        self.r = []


class Sched:
    ENGINES = ("tensor", "vector", "scalar", "sync", "gpsimd")
    DMA_SEMS = ("lds", "ldw", "ldx", "wt", "out", "tr1", "tr2", "tr3")

    def __init__(self):
        self.ops = []
        self.last_of_dma = {d: None for d in self.DMA_SEMS}
        self.own = {"tensor": "pe", "vector": "dve", "scalar": "act",
                    "sync": "spq", "gpsimd": "gpq"}

    def do(self, engine, emit, rd=(), wr=(), dma_sem=None, keep_pe=False, also=()):
        oid = len(self.ops)
        deps = set(also)
        if dma_sem is not None and self.last_of_dma[dma_sem] is not None:
            deps.add(self.last_of_dma[dma_sem])
        for b in rd:
            if b.w is not None:
                deps.add(b.w)
        for b in wr:
            if b.w is not None:
                deps.add(b.w)
            deps.update(b.r)
        self.ops.append((engine, emit, deps, dma_sem, keep_pe))
        if dma_sem is not None:
            self.last_of_dma[dma_sem] = oid
        for b in rd:
            b.r.append(oid)
        for b in wr:
            b.w = oid
            b.r = []
        return oid

    def emit(self, block, sems, final_waits=()):
        n = len(self.ops)
        engine_of = [o[0] for o in self.ops]
        dma_of = [o[3] for o in self.ops]
        keep_of = [o[4] for o in self.ops]
        sem_of = [dma_of[i] if dma_of[i] else self.own[engine_of[i]]
                  for i in range(n)]
        awaited = [False] * n
        for i, (eng, _e, deps, _d, _k) in enumerate(self.ops):
            for d in deps:
                if engine_of[i] == "tensor" and engine_of[d] == "tensor" \
                        and dma_of[d] is None and not keep_of[i]:
                    continue
                awaited[d] = True
        for d in final_waits:
            awaited[d] = True
        val = [None] * n
        cum = {}
        for i in range(n):
            if dma_of[i] is not None:
                awaited[i] = True
            if awaited[i]:
                sm = sem_of[i]
                cum[sm] = cum.get(sm, 0) + (16 if dma_of[i] else 1)
                val[i] = cum[sm]
        qs = {e: [] for e in self.ENGINES}
        last_wait = {e: {} for e in self.ENGINES}
        for i, (eng, emitfn, deps, dma, _k) in enumerate(self.ops):
            need = {}
            for d in deps:
                if engine_of[i] == "tensor" and engine_of[d] == "tensor" \
                        and dma_of[d] is None and not keep_of[i]:
                    continue
                sm, v = sem_of[d], val[d]
                assert v is not None
                if need.get(sm, 0) < v:
                    need[sm] = v
            waits = []
            lw = last_wait[eng]
            for sm, v in need.items():
                if lw.get(sm, 0) < v:
                    waits.append((sm, v))
                    lw[sm] = v
            inc = (sem_of[i], 16 if dma else 1) if awaited[i] else None
            qs[eng].append((waits, emitfn, inc))

        def mk(ops):
            def _(eng):
                for waits, emitfn, inc in ops:
                    for sm, v in waits:
                        eng.wait_ge(sems[sm], v)
                    instr = emitfn(eng)
                    if inc is not None:
                        instr.then_inc(sems[inc[0]], inc[1])
            return _
        for e in self.ENGINES:
            if qs[e]:
                getattr(block, e)(mk(qs[e]))
        return {d: (sem_of[d], val[d]) for d in final_waits}


def build_nc():
    nc = bass.Bass("TRN2", target_bir_lowering=False)

    xin = nc.dram_tensor("xin", [NT * 128, D], F32, kind="ExternalInput").ap()
    wq = nc.dram_tensor("wq", [D, D], F32, kind="ExternalInput").ap()
    wk = nc.dram_tensor("wk", [D, D], F32, kind="ExternalInput").ap()
    wv = nc.dram_tensor("wv", [D, D], F32, kind="ExternalInput").ap()
    wo = nc.dram_tensor("wo", [D, D], F32, kind="ExternalInput").ap()
    tern = nc.dram_tensor("tern", [D, D], I32, kind="ExternalInput").ap()
    bqh = nc.dram_tensor("bqh", [128, D], BF16, kind="ExternalInput").ap()
    bkh = nc.dram_tensor("bkh", [128, D], BF16, kind="ExternalInput").ap()
    bvh = nc.dram_tensor("bvh", [128, D], BF16, kind="ExternalInput").ap()
    boh = nc.dram_tensor("boh", [128, D], BF16, kind="ExternalInput").ap()
    bfh = nc.dram_tensor("bfh", [128, D], BF16, kind="ExternalInput").ap()
    ash = nc.dram_tensor("ash", [128, 1], F32, kind="ExternalInput").ap()
    ckh = nc.dram_tensor("ckh", [NT * 128, DH], BF16, kind="ExternalInput").ap()
    skh = nc.dram_tensor("skh", [NT * 128, DH], BF16, kind="ExternalInput").ap()
    msh = nc.dram_tensor("msh", [128, 256], BF16, kind="ExternalInput").ap()
    mj0h = nc.dram_tensor("mj0h", [128, 128], BF16, kind="ExternalInput").ap()
    mA1h = nc.dram_tensor("mA1h", [128, 256], BF16, kind="ExternalInput").ap()
    idfh = nc.dram_tensor("idfh", [128, 128], F32, kind="ExternalInput").ap()
    idbh = nc.dram_tensor("idbh", [128, 128], BF16, kind="ExternalInput").ap()
    eohh = nc.dram_tensor("eohh", [16, 16 * 128], BF16, kind="ExternalInput").ap()
    xout = nc.dram_tensor("xout", [SC, D], F32, kind="ExternalOutput").ap()

    s = Sched()
    ctx = []

    def sb(name, shape, dt):
        t = nc.sbuf_tensor(name, shape, dt)
        ctx.append(t)
        return t.__enter__()

    def ps(name, shape, dt):
        t = nc.psum_tensor(name, shape, dt)
        ctx.append(t)
        return t.__enter__()

    XR = 6
    xs = [sb(f"xs{i}", [128, D], F32) for i in range(XR)]
    xnst = [sb("xnst0", [128, D], BF16)] * 2
    xnT = [sb(f"xnT{i}", [128, D], BF16) for i in range(2)]
    qb = sb("qb", [128, D], BF16)
    t2 = sb("t2", [128, D], BF16)
    q_tm = [sb("qtm0", [128, D], BF16)] * 2
    k_tm = [sb("ktm0", [128, D], BF16)] * 2
    qTd = [sb(f"qTd{i}", [128, 2 * D], BF16) for i in range(3)]
    kTd = [sb(f"kTd{i}", [128, 2 * D], BF16) for i in range(3)]
    vaug = [sb(f"vaug{i}", [128, H * 65], BF16) for i in range(4)]
    expT = [sb(f"expT{i}", [128, H * 256], BF16) for i in range(2)]
    attn = [sb("attn0", [128, D], BF16)] * 2
    attT = [sb("attT0", [128, D], BF16)] * 2
    xn2 = sb("xn2", [128, D], F32)
    xint = [sb("xint0", [128, D], BF16)] * 2
    xiT = [sb("xiT0", [128, D], BF16)] * 2
    wqT = sb("wqT", [128, 8 * D], BF16)
    wkT = sb("wkT", [128, 8 * D], BF16)
    wvT = sb("wvT", [128, 8 * D], BF16)
    woT = sb("woT", [128, 8 * D], BF16)
    wfT = sb("wfT", [128, 8 * D], BF16)
    bq = sb("bq", [128, D], BF16)
    bk = sb("bk", [128, D], BF16)
    bv = sb("bv", [128, D], BF16)
    bo = sb("bo", [128, D], BF16)
    bf_ = sb("bf_", [128, D], BF16)
    as128 = sb("as128", [128, 1], F32)
    smax = sb("smax", [128, 1], F32)
    rs128 = sb("rs128", [128, 1], F32)
    ck = sb("ck", [128, NT * DH], BF16)
    sk_ = sb("sk_", [128, NT * DH], BF16)
    mstd = sb("mstd", [128, 256], BF16)
    mj0 = sb("mj0", [128, 128], BF16)
    mA1 = sb("mA1", [128, 256], BF16)
    idf = sb("idf", [128, 128], F32)
    idb = sb("idb", [128, 128], BF16)
    eoh = sb("eoh", [16, 16 * 128], BF16)
    nmT = sb("nmT", [16, NT * 128], BF16)
    nmt = sb("nmt", [128, 16], F32)
    bna = sb("bna", [128, 2], F32)
    bns = sb("bns", [128, 12], F32)
    lnt = sb("lnt", [128, 4], F32)
    bna2 = sb("bna2", [128, 2], F32)
    bns2 = sb("bns2", [128, 12], F32)
    lnt2 = sb("lnt2", [128, 4], F32)
    den = sb("den", [128, 16], F32)
    rec = sb("rec", [128, 16], F32)
    cst = sb("cst", [128, 2], F32)
    wstg = sb("wstg", [128, 2 * D], BF16)
    nc.const_aps.register(F32, EPS, cst[:, 0:1])
    nc.const_aps.register(F32, -0.5, cst[:, 1:2])

    bank = [ps(f"pb{i}", [128, 512], F32) for i in range(8)]

    bXS = [Buf() for _ in range(XR)]
    _bxn = Buf(); bXNS = [_bxn, _bxn]
    bXNT = [Buf() for _ in range(2)]
    bQB, bT2 = Buf(), Buf()
    _bq = Buf(); _bk = Buf()
    bQTM = [_bq, _bq]
    bKTM = [_bk, _bk]
    bQTdL = [Buf() for _ in range(3)]
    bQTdR = [Buf() for _ in range(3)]
    bKTdL = [Buf() for _ in range(3)]
    bKTdR = [Buf() for _ in range(3)]
    bVA = [Buf() for _ in range(4)]
    bEXPA = [Buf() for _ in range(2)]
    bEXPB = [Buf() for _ in range(2)]
    _ba = Buf(); bATTN = [_ba, _ba]
    _bat = Buf(); bATTT = [_bat, _bat]
    bXN2 = Buf()
    _bxi = Buf(); bXINT = [_bxi, _bxi]
    _bxt2 = Buf(); bXIT = [_bxt2, _bxt2]
    bW = {n: [Buf() for _ in range(4)] for n in ("wq", "wk", "wv", "wo", "wf")}
    bWSTG = Buf()
    bSMALL = Buf()
    bNMT = [Buf() for _ in range(NT)]
    bNM = Buf()
    bLN = Buf()
    bLN2 = Buf()
    bDEN = Buf()
    bCONST = Buf()
    bBANK = [Buf() for _ in range(8)]
    bSC = Buf()

    wT = {"wq": wqT, "wk": wkT, "wv": wvT, "wo": woT, "wf": wfT}

    trn = [0]

    def dma_T(dst_ap, src_ap, rd, wr):
        sem = ("tr1", "tr1", "tr1")[trn[0] % 3]
        eng = "scalar" if sem == "tr3" else "sync"
        trn[0] += 1
        s.do(eng, lambda e, d=dst_ap, sr=src_ap: e.dma_start_transpose(d, sr),
             rd=rd, wr=wr, dma_sem=sem)


    s.do("vector", lambda e: e.memset(cst[:, 0:1], EPS), wr=[bCONST])
    s.do("vector", lambda e: e.memset(cst[:, 1:2], -0.5), wr=[bCONST])
    s.do("vector", lambda e: e.memset(nmT[:], 0.0), wr=[bNMT[0]])
    for i in range(4):
        s.do("vector", lambda e, i=i: e.memset(
            vaug[i][:].rearrange("p (h c) -> p h c", h=H)[:, :, 64:65], 1.0),
            wr=[bVA[i]])

    qctr = [0]

    def prep_quarter(name, src, hf):
        dst = wT[name]
        qi = qctr[0]
        qctr[0] += 1
        if qi % 2 == 0:
            stg = [wstg[:, 0:1024], wstg[:, 1024:2048]]
            bstg = [bWSTG, bWSTG]
        else:
            stg = [expT[1][:, 2048:3072], expT[1][:, 3072:4096]]
            bstg = [bEXPB[1], bEXPB[1]]
        for g in range(2):
            s.do("scalar", lambda e, hf=hf, g=g, src=src: e.dma_start(
                out=xs[4 + g][:],
                in_=src[hf * 256 + g * 128: hf * 256 + (g + 1) * 128, :]),
                wr=[bXS[4 + g]], dma_sem=("wt", "ldw")[(2 * qi + g) % 2])
        for g in range(2):
            if name == "wq":
                s.do("vector", lambda e, g=g, stg=stg: e.tensor_scalar_mul(
                    stg[g], xs[4 + g][:], 0.125),
                    rd=[bXS[4 + g]], wr=[bstg[g]])
            else:
                s.do("vector", lambda e, g=g, stg=stg: e.tensor_copy(
                    stg[g], xs[4 + g][:]),
                    rd=[bXS[4 + g]], wr=[bstg[g]])
        for og4 in range(2):
            og = hf * 2 + og4
            s.do("sync", lambda e, og=og, og4=og4, dst=dst, stg=stg: e.dma_start_transpose(
                dst[:].rearrange("p (g f) -> p g f", g=8)[:, :, og * 128:(og + 1) * 128],
                stg[og4].rearrange("p (g f) -> p g f", g=8)),
                rd=[bstg[og4]], wr=[bW[name][hf]], dma_sem="tr1")

    s.do("sync", lambda e: e.dma_start(out=xs[0][:], in_=xin[0:128, :]),
         wr=[bXS[0]], dma_sem="ldx")
    s.do("sync", lambda e: e.dma_start(out=xs[1][:], in_=xin[128:256, :]),
         wr=[bXS[1]], dma_sem="ldx")
    def ln_stats(xbuf, bx, bnsX, bnaX, lntX, bL):
        s.do("vector", lambda e: e.bn_stats(bnsX[:, 0:6], xbuf[:, 0:512]), rd=[bx], wr=[bL])
        s.do("vector", lambda e: e.bn_stats(bnsX[:, 6:12], xbuf[:, 512:1024]), rd=[bx], wr=[bL])
        s.do("vector", lambda e: e.bn_aggr(bnaX[:], bnsX[:]), rd=[bL], wr=[bL])
        s.do("scalar", lambda e: e.activation(lntX[:, 2:3], bnaX[:, 1:2], AF.Ln, bias=EPS),
             rd=[bL, bCONST], wr=[bL])
        s.do("scalar", lambda e: e.activation(lntX[:, 0:1], lntX[:, 2:3], AF.Exp, scale=-0.5),
             rd=[bL, bCONST], wr=[bL])
        s.do("vector", lambda e: e.tensor_scalar(lntX[:, 1:2], bnaX[:, 0:1], lntX[:, 0:1],
                                                 -1.0, op0=ALU.mult, op1=ALU.mult),
             rd=[bL], wr=[bL])

    def ln_apply(xbuf, bx, out, bout, lntX, bL):
        s.do("scalar", lambda e: e.activation(out, xbuf[:], AF.Identity,
                                              bias=lntX[:, 1:2], scale=lntX[:, 0:1]),
             rd=[bx, bL], wr=[bout])

    def rope(proj_banks, bias_sb, j, outbuf, bout):
        pb0, pb1 = proj_banks
        s.do("vector", lambda e: e.scalar_tensor_tensor(
            qb[:, 0:512], bank[pb0][:], 1.0, bias_sb[:, 0:512], op0=ALU.mult, op1=ALU.add),
            rd=[bBANK[pb0], bSMALL], wr=[bQB])
        s.do("vector", lambda e: e.scalar_tensor_tensor(
            qb[:, 512:1024], bank[pb1][:], 1.0, bias_sb[:, 512:1024], op0=ALU.mult, op1=ALU.add),
            rd=[bBANK[pb1], bSMALL], wr=[bQB])
        qb3 = qb[:].rearrange("p (h t f) -> p h t f", h=H, t=2)
        c3 = ck[:, j * 64:(j + 1) * 64].rearrange("p (t f) -> p t f", t=2) \
            .unsqueeze(1).broadcast_to([128, H, 2, 32])
        s3 = sk_[:, j * 64:(j + 1) * 64].rearrange("p (t f) -> p t f", t=2) \
            .unsqueeze(1).broadcast_to([128, H, 2, 32])
        o3 = outbuf[:].rearrange("p (h t f) -> p h t f", h=H, t=2)
        t23 = t2[:].rearrange("p (h t f) -> p h t f", h=H, t=2)
        s.do("vector", lambda e: e.tensor_tensor(t23[:, :, 0:1, :], qb3[:, :, 1:2, :],
                                                 s3[:, :, 0:1, :], op=ALU.mult),
             rd=[bQB, bSMALL], wr=[bT2])
        s.do("vector", lambda e: e.tensor_tensor(t23[:, :, 1:2, :], qb3[:, :, 0:1, :],
                                                 s3[:, :, 1:2, :], op=ALU.mult),
             rd=[bQB, bSMALL], wr=[bT2])
        s.do("vector", lambda e: e.tensor_tensor(o3, qb3, c3, op=ALU.mult),
             rd=[bQB, bSMALL], wr=[bout])
        s.do("vector", lambda e: e.tensor_tensor(outbuf[:], outbuf[:], t2[:], op=ALU.add),
             rd=[bT2], wr=[bout])

    def tr_tile(src, bsrc, dst_ap, bw):
        dma_T(dst_ap.rearrange("p (g f) -> p g f", g=8),
              src[:].rearrange("p (g f) -> p g f", g=8), rd=[bsrc], wr=[bw])

    def hsl(t_, h, half=None, width=128):
        p0 = (h % 2) * 64
        c0 = (h // 2) * width + (0 if half is None else half * 1024)
        return t_[p0:p0 + 64, c0:c0 + width]

    def ln1(t):
        xsl, bx = xs[t % XR], bXS[t % XR]
        ln_stats(xsl, bx, bns, bna, lnt, bLN)
        ln_apply(xsl, bx, xnst[t % 2][:], bXNS[t % 2], lnt, bLN)
        tr_tile(xnst[t % 2], bXNS[t % 2], xnT[t % 2][:], bXNT[t % 2])

    ln1(0)
    _ssem = ("lds", "out", "tr3")
    for si, (dst, src) in enumerate(((bq, bqh), (bk, bkh), (bv, bvh), (bo, boh), (bf_, bfh),
                                     (mstd, msh), (mj0, mj0h), (mA1, mA1h),
                                     (idf, idfh), (idb, idbh), (eoh, eohh), (as128, ash))):
        s.do("scalar", lambda e, d=dst, sr=src: e.dma_start(out=d[:], in_=sr),
             wr=[bSMALL], dma_sem=_ssem[si % 3])
    for si, (dst, src) in enumerate(((ck, ckh), (sk_, skh))):
        s.do("scalar", lambda e, d=dst, sr=src: e.dma_start(
            out=d[:].rearrange("p (t f) -> p t f", t=NT),
            in_=sr.rearrange("(t p) f -> p t f", p=128)),
            wr=[bSMALL], dma_sem=_ssem[si % 3])
    s.do("vector", lambda e: e.tensor_scalar_max(smax[:], as128[:], 1e-5),
         rd=[bSMALL], wr=[bSC])
    s.do("vector", lambda e: e.reciprocal(rs128[:], smax[:]), rd=[bSC], wr=[bSC])
    s.do("vector", lambda e: e.tensor_scalar_mul(bq[:], bq[:], 0.125),
         rd=[bSMALL], wr=[bSMALL])
    for hf in range(4):
        prep_quarter("wq", wq, hf)
        prep_quarter("wk", wk, hf)
        prep_quarter("wv", wv, hf)
    for hf in range(4):
        prep_quarter("wo", wo, hf)
    def wf_prep():
        wstgF = expT[1]
        bSTF = [bEXPA[1], bEXPB[1]]
        for hf in range(4):
            s.do("gpsimd", lambda e, hf=hf: e.dma_start(
                out=wstgF[:, 0:2048].rearrange("p (g f) -> p g f", g=2),
                in_=tern[hf * 256:(hf + 1) * 256, :].rearrange("(g p) f -> p g f", p=128)),
                rd=bSTF, wr=bSTF, dma_sem="tr2")
            s.do("vector", lambda e: e.tensor_scalar_add(wstgF[:, 0:2048], wstgF[:, 0:2048], -1.0),
                 rd=bSTF, wr=bSTF)
            for og4 in range(2):
                og = hf * 2 + og4
                s.do("sync", lambda e, og=og, og4=og4: e.dma_start_transpose(
                    wT["wf"][:].rearrange("p (g f) -> p g f", g=8)[:, :, og * 128:(og + 1) * 128],
                    wstgF[:, og4 * 1024:(og4 + 1) * 1024].rearrange("p (g f) -> p g f", g=8)),
                    rd=bSTF, wr=[bW["wf"][hf]], dma_sem="tr1")


    for j in range(0, NT + 4):
        if j == 1:
            wf_prep()

        t = j - 1
        doA = 1 <= t <= 16
        doQ = j <= 16
        if doA:
            qd, bqdL = qTd[t % 3], bQTdL[t % 3]
            kd = kTd[t % 3]
            bkdL, bkdR = bKTdL[t % 3], bKTdR[t % 3]

            def passA_group(hg, t=t, qd=qd, kd=kd, bqdL=bqdL, bkdL=bkdL, bkdR=bkdR):
                slot = (hg % 2) * 256
                for hh in range(2):
                    h = hg * 2 + hh
                    pb_ = 6 + hh
                    rhs = kd[(h % 2) * 64:(h % 2) * 64 + 64, :] \
                        .rearrange("p (a c) -> p a c", a=2)[:, :, (h // 2) * 128:(h // 2) * 128 + 128]
                    s.do("tensor", lambda e, h=h, slot=slot, pb_=pb_, qd=qd, rhs=rhs:
                         e.matmul(bank[pb_][:, slot:slot + 256],
                                  hsl(qd, h, 0), rhs,
                                  start=True, stop=True),
                         rd=[bqdL, bkdL, bkdR], wr=[bBANK[pb_]])
                for hh in range(2):
                    h = hg * 2 + hh
                    pb_ = 6 + hh
                    if t == 1:
                        s.do("vector", lambda e, pb_=pb_, slot=slot: e.tensor_tensor(
                            bank[pb_][:, slot:slot + 256], bank[pb_][:, slot:slot + 256],
                            mA1[:], op=ALU.add),
                            rd=[bSMALL], wr=[bBANK[pb_]])
                    s.do("vector", lambda e, h=h, pb_=pb_, slot=slot: e.reduce_max(
                        nmt[:, h:h + 1], bank[pb_][:, slot:slot + 256],
                        axis=AX.X, negate=True),
                        rd=[bBANK[pb_]], wr=[bNM])
        if doQ:
            xnsl, bxnsl = xnT[j % 2], bXNT[j % 2]

            def proj_mms(wt_, bw_, pb0, pb1, inter=None):
                for dg in range(8):
                    lhs = xnsl[:, dg * 128:(dg + 1) * 128]
                    for half, pb_ in ((0, pb0), (1, pb1)):
                        s.do("tensor", lambda e, lhs=lhs, wt_=wt_, dg=dg, half=half, pb_=pb_, st=(dg == 0), sp=(dg == 7):
                             e.matmul(bank[pb_][:], lhs,
                                      wt_[:, dg * 1024 + half * 512: dg * 1024 + (half + 1) * 512],
                                      start=st, stop=sp),
                             rd=[bxnsl, bw_[2 * half], bw_[2 * half + 1]], wr=[bBANK[pb_]])
                    if inter is not None and dg % 2 == 1:
                        passA_group(inter + dg // 2)

        if doQ:
            proj_mms(wvT, bW["wv"], 4, 5, inter=(0 if doA else None))
        elif doA:
            for hg in range(4):
                passA_group(hg)
        if doQ:
            va, bva = vaug[j % 4], bVA[j % 4]
            va3 = va[:].rearrange("p (h c) -> p h c", h=H)[:, :, 0:64]
            bv3 = bv[:].rearrange("p (h c) -> p h c", h=H)
            for half, pb_ in ((0, 4), (1, 5)):
                s.do("vector", lambda e, half=half, pb_=pb_, va3=va3, bv3=bv3: e.scalar_tensor_tensor(
                    va3[:, half * 8:(half + 1) * 8, :],
                    bank[pb_][:].rearrange("p (h c) -> p h c", h=8),
                    1.0, bv3[:, half * 8:(half + 1) * 8, :],
                    op0=ALU.mult, op1=ALU.add),
                    rd=[bBANK[pb_], bSMALL], wr=[bva])
        if doQ:
            proj_mms(wqT, bW["wq"], 0, 1, inter=(4 if doA else None))
        elif doA:
            for hg in range(4):
                passA_group(4 + hg)

        if 1 <= j - 1 <= 16:
            t = j - 1
            s.do("tensor", lambda e: e.transpose(bank[6][0:16, 0:128], nmt[:], idf[:]),
                 rd=[bNM, bSMALL], wr=[bBANK[6]])
            s.do("vector", lambda e, t=t: e.tensor_copy(nmT[0:16, t * 128:(t + 1) * 128],
                                                        bank[6][0:16, 0:128]),
                 rd=[bBANK[6]], wr=[bNMT[t]])

        if doQ and j >= 1:
            rope((0, 1), bq, j, q_tm[j % 2], bQTM[j % 2])
            tr_tile(q_tm[j % 2], bQTM[j % 2], qTd[j % 3][:][:, 0:1024], bQTdL[j % 3])
            tr_tile(q_tm[j % 2], bQTM[j % 2], qTd[(j - 1) % 3][:][:, 1024:2048], bQTdR[(j - 1) % 3])

        if doQ:
            proj_mms(wkT, bW["wk"], 2, 3)
            rope((2, 3), bk, j, k_tm[j % 2], bKTM[j % 2])
            tr_tile(k_tm[j % 2], bKTM[j % 2], kTd[j % 3][:][:, 1024:2048], bKTdR[j % 3])
            if j + 1 <= 16:
                tr_tile(k_tm[j % 2], bKTM[j % 2], kTd[(j + 1) % 3][:][:, 0:1024], bKTdL[(j + 1) % 3])

        if j + 1 <= 16:
            ln1(j + 1)

        EBP = {0: 0, 2: 1, 1: 2, 3: 3, 4: 4, 6: 5, 5: 6, 7: 7}
        eb = lambda h: (h // 8) * 8 + EBP[h % 8]

        if 0 <= j - 2 <= 16:
            t = j - 2
            ebuf = expT[t % 2]
            kd = kTd[t % 3]
            bkdR = bKTdR[t % 3]
            qd, bqdL, bqdR = qTd[t % 3], bQTdL[t % 3], bQTdR[t % 3]
            first_half = t >= 1
            second_half = t <= 15
            for hg8 in (1, 0):
                pbs = (4, 5, 6, 7) if hg8 == 0 else (0, 1, 2, 3)
                beb = (bEXPA if hg8 == 0 else bEXPB)[t % 2]
                def hb(h):
                    q = (h % 8) // 2
                    return pbs[2 * (q // 2) + (h % 2)], (q % 2) * 256
                for q4 in range(4):
                    for hh in range(2):
                        h = hg8 * 8 + q4 * 2 + hh
                        pb_, slot = hb(h)
                        if first_half and second_half:
                            rhs = qd[(h % 2) * 64:(h % 2) * 64 + 64, :] \
                                .rearrange("p (a c) -> p a c", a=2)[:, :, (h // 2) * 128:(h // 2) * 128 + 128]
                            o0, cw, c0 = 0, 256, t * 128
                            rdq = [bqdL, bqdR]
                        elif second_half:
                            rhs = hsl(qd, h, 1)
                            o0, cw, c0 = 128, 128, (t + 1) * 128
                            rdq = [bqdR]
                        else:
                            rhs = hsl(qd, h, 0)
                            o0, cw, c0 = 0, 128, t * 128
                            rdq = [bqdL]
                        sc_last = s.do("tensor", lambda e, h=h, slot=slot, pb_=pb_, kd=kd, rhs=rhs, o0=o0, cw=cw, st=(slot == 0):
                             e.matmul(bank[pb_][:, slot + o0: slot + o0 + cw],
                                      hsl(kd, h, 1), rhs,
                                      start=st, stop=False),
                             rd=[bkdR] + rdq, wr=[bBANK[pb_]])
                rdl = [bSMALL]
                if first_half:
                    rdl.append(bNMT[t])
                if second_half:
                    rdl.append(bNMT[t + 1])
                for q4 in range(4):
                    for hh in range(2):
                        h = hg8 * 8 + q4 * 2 + hh
                        pb_, slot = hb(h)
                        s.do("tensor", lambda e, h=h, slot=slot, pb_=pb_, c0=c0, cw=cw, o0=o0:
                             e.matmul(bank[pb_][:, slot + o0: slot + o0 + cw],
                                      eoh[:, h * 128:(h + 1) * 128],
                                      nmT[0:16, c0:c0 + cw],
                                      start=False, stop=False),
                             rd=rdl, wr=[bBANK[pb_]],
                             keep_pe=(q4 == 0 and hh == 0), also=([sc_last] if (q4 == 0 and hh == 0) else ()))
                if first_half and second_half:
                    mrhs = mstd[:]
                elif second_half:
                    mrhs = mj0[:]
                else:
                    mrhs = mstd[:, 0:128]
                for q4 in range(4):
                    for hh in range(2):
                        h = hg8 * 8 + q4 * 2 + hh
                        pb_, slot = hb(h)
                        s.do("tensor", lambda e, pb_=pb_, slot=slot, o0=o0, cw=cw, mrhs=mrhs, sp=(slot != 0):
                             e.matmul(bank[pb_][:, slot + o0: slot + o0 + cw],
                                      idb[:], mrhs,
                                      start=False, stop=sp),
                             rd=[bSMALL], wr=[bBANK[pb_]])
                if first_half and second_half:
                    for q2 in range(4):
                        hA = hg8 * 8 + (0, 1, 4, 5)[q2]
                        pb_, _ = hb(hA)
                        e0 = eb(hA) * 256
                        s.do("scalar", lambda e, pb_=pb_, e0=e0, ebuf=ebuf: e.activation(
                            ebuf[:, e0:e0 + 512], bank[pb_][:, 0:512], AF.Exp),
                            rd=[bBANK[pb_]], wr=[beb])
                else:
                    sl0, sl1 = (128, 256) if second_half else (0, 128)
                    for h in range(hg8 * 8, hg8 * 8 + 8):
                        pb_, slot = hb(h)
                        s.do("scalar", lambda e, pb_=pb_, slot=slot, h=h, sl0=sl0, sl1=sl1, ebuf=ebuf: e.activation(
                            ebuf[:, eb(h) * 256 + sl0:eb(h) * 256 + sl1],
                            bank[pb_][:, slot + sl0:slot + sl1], AF.Exp),
                            rd=[bBANK[pb_]], wr=[beb])

        if 1 <= j - 4 <= 16:
            t = j - 4
            xT_, bxT_ = xiT[t % 2], bXIT[t % 2]
            for dg in range(8):
                for half, pb_ in ((0, 0), (1, 1)):
                    s.do("tensor", lambda e, dg=dg, half=half, pb_=pb_, xT_=xT_:
                         e.matmul(bank[pb_][:], xT_[:, dg * 128:(dg + 1) * 128],
                                  wfT[:, dg * 1024 + half * 512: dg * 1024 + (half + 1) * 512],
                                  start=(dg == 0), stop=(dg == 7)),
                         rd=[bxT_, bW["wf"][2 * half], bW["wf"][2 * half + 1]], wr=[bBANK[pb_]])
            xt, bxt = xs[t % XR], bXS[t % XR]
            s.do("vector", lambda e, xt=xt: e.tensor_tensor(xt[:], xt[:], bf_[:], op=ALU.add),
                 rd=[bSMALL, bxt], wr=[bxt])
            for half, pb_ in ((0, 0), (1, 1)):
                s.do("vector", lambda e, half=half, pb_=pb_, xt=xt: e.scalar_tensor_tensor(
                    xt[:, half * 512:(half + 1) * 512], bank[pb_][:], smax[:],
                    xt[:, half * 512:(half + 1) * 512], op0=ALU.mult, op1=ALU.add),
                    rd=[bBANK[pb_], bSC], wr=[bxt])
            s.do("scalar", lambda e, t=t, xt=xt: e.dma_start(
                out=xout[(t - 1) * 128: t * 128, :], in_=xt[:]),
                rd=[bxt], dma_sem="out")

        if 1 <= j - 2 <= 16:
            t = j - 2
            em1 = expT[(t - 1) % 2]
            em0 = expT[t % 2]
            va1, bva1 = vaug[(t - 1) % 4], bVA[(t - 1) % 4]
            va0, bva0 = vaug[t % 4], bVA[t % 4]
            groups = ((4, 0, tuple(range(0, 7))), (5, 0, tuple(range(7, 14))), (7, 0, (14, 15)))
            for pb_, off, hs in groups:
                for i, h in enumerate(hs):
                    c = off + (h - hs[0]) * 65
                    b1a, b1b = bEXPA[(t - 1) % 2], bEXPB[(t - 1) % 2]
                    b0a, b0b = bEXPA[t % 2], bEXPB[t % 2]
                    s.do("tensor", lambda e, h=h, c=c, pb_=pb_, st=(i == 0), em1=em1, va1=va1:
                         e.matmul(bank[pb_][:, c:c + 65],
                                  em1[:, eb(h) * 256 + 128: eb(h) * 256 + 256],
                                  va1[:, h * 65:(h + 1) * 65],
                                  start=st, stop=False),
                         rd=[b1a, b1b, bva1], wr=[bBANK[pb_]])
                    s.do("tensor", lambda e, h=h, c=c, pb_=pb_, sp=(i == len(hs) - 1), em0=em0, va0=va0:
                         e.matmul(bank[pb_][:, c:c + 65],
                                  em0[:, eb(h) * 256: eb(h) * 256 + 128],
                                  va0[:, h * 65:(h + 1) * 65],
                                  start=False, stop=sp),
                         rd=[b0a, b0b, bva0], wr=[bBANK[pb_]])
            for pb_, off, hs in groups:
                n = len(hs)
                s.do("scalar", lambda e, pb_=pb_, off=off, h0=hs[0], n=n: e.activation(
                    den[:, h0:h0 + n],
                    bank[pb_][:, off:off + 65 * n].rearrange("p (a b) -> p a b", a=n)[:, :, 64:65].squeeze(2),
                    AF.Identity),
                    rd=[bBANK[pb_]], wr=[bDEN])
            s.do("vector", lambda e: e.reciprocal(rec[:], den[:]), rd=[bDEN], wr=[bDEN])
            at, bat = attn[t % 2], bATTN[t % 2]
            for pb_, off, hs in groups:
                for h in hs:
                    c = off + (h - hs[0]) * 65
                    s.do("scalar", lambda e, h=h, c=c, pb_=pb_, at=at: e.activation(
                        at[:, h * 64:(h + 1) * 64], bank[pb_][:, c:c + 64], AF.Identity,
                        scale=rec[:, h:h + 1]),
                        rd=[bBANK[pb_], bDEN], wr=[bat])

        if 1 <= j - 3 <= 16:
            t = j - 3
            aT, baT = attT[t % 2], bATTT[t % 2]
            for dg in range(8):
                for half, pb_ in ((0, 1), (1, 2)):
                    s.do("tensor", lambda e, dg=dg, half=half, pb_=pb_, aT=aT:
                         e.matmul(bank[pb_][:], aT[:, dg * 128:(dg + 1) * 128],
                                  woT[:, dg * 1024 + half * 512: dg * 1024 + (half + 1) * 512],
                                  start=(dg == 0), stop=(dg == 7)),
                         rd=[baT, bW["wo"][2 * half], bW["wo"][2 * half + 1]], wr=[bBANK[pb_]])
            xt, bxt = xs[t % XR], bXS[t % XR]
            s.do("vector", lambda e, xt=xt: e.tensor_tensor(xt[:], xt[:], bo[:], op=ALU.add),
                 rd=[bSMALL, bxt], wr=[bxt])
            for half, pb_ in ((0, 1), (1, 2)):
                s.do("vector", lambda e, half=half, pb_=pb_, xt=xt: e.scalar_tensor_tensor(
                    xt[:, half * 512:(half + 1) * 512], bank[pb_][:], 1.0,
                    xt[:, half * 512:(half + 1) * 512], op0=ALU.mult, op1=ALU.add),
                    rd=[bBANK[pb_]], wr=[bxt])
            ln_stats(xt, bxt, bns2, bna2, lnt2, bLN2)
            ln_apply(xt, bxt, xn2[:], bXN2, lnt2, bLN2)
            s.do("vector", lambda e: e.tensor_scalar(xn2[:], xn2[:], rs128[:], None, op0=ALU.mult),
                 rd=[bSC], wr=[bXN2])
            s.do("vector", lambda e: e.tensor_scalar(xn2[:], xn2[:], MAGIC, MAGIC,
                                                     op0=ALU.add, op1=ALU.subtract),
                 rd=[], wr=[bXN2])
            s.do("vector", lambda e, t=t: e.tensor_scalar(xint[t % 2][:], xn2[:], -127.0, 127.0,
                                                          op0=ALU.max, op1=ALU.min),
                 rd=[bXN2], wr=[bXINT[t % 2]])
            tr_tile(xint[t % 2], bXINT[t % 2], xiT[t % 2][:], bXIT[t % 2])

        if 1 <= j - 2 <= 16:
            ta_ = j - 2
            tr_tile(attn[ta_ % 2], bATTN[ta_ % 2], attT[ta_ % 2][:], bATTT[ta_ % 2])

        if 2 <= j + 2 <= NT - 1:
            s.do("sync", lambda e, t=j + 2: e.dma_start(
                out=xs[t % XR][:], in_=xin[t * 128:(t + 1) * 128, :]),
                wr=[bXS[(j + 2) % XR]], dma_sem="ldx")

    final_out = s.last_of_dma["out"]

    with (
        nc.semaphore("pe") as sem_pe,
        nc.semaphore("dve") as sem_dve,
        nc.semaphore("act") as sem_act,
        nc.semaphore("spq") as sem_spq,
        nc.semaphore("gpq") as sem_gpq,
        nc.semaphore("lds") as sem_lds,
        nc.semaphore("ldw") as sem_ldw,
        nc.semaphore("ldx") as sem_ldx,
        nc.semaphore("wt") as sem_wt,
        nc.semaphore("outq") as sem_out,
        nc.semaphore("tr1") as sem_tr1,
        nc.semaphore("tr2") as sem_tr2,
        nc.semaphore("tr3") as sem_tr3,
        nc.Block() as block,
    ):
        sems = {"pe": sem_pe, "dve": sem_dve, "act": sem_act,
                "spq": sem_spq, "gpq": sem_gpq, "lds": sem_lds,
                "ldw": sem_ldw, "ldx": sem_ldx, "wt": sem_wt, "out": sem_out,
                "tr1": sem_tr1, "tr2": sem_tr2, "tr3": sem_tr3}
        s.ops.append(("sync", lambda e: e.nop(), {final_out}, None, False))
        s.emit(block, sems)

    for t_ in reversed(ctx):
        t_.__exit__(None, None, None)
    return nc


def _tables(h):
    if h == 0:
        pos = np.concatenate([np.zeros(128), np.arange(0, SC)])
    else:
        pos = np.arange(SC - 128, SC + SC)
    pos = pos.astype(np.float64)
    inv = 1.0 / (10000.0 ** (np.arange(32, dtype=np.float64) / 32.0))
    ang = pos[:, None] * inv[None, :]
    c = np.cos(ang).astype(np.float32)
    sn = np.sin(ang).astype(np.float32)
    c64 = np.concatenate([c, c], axis=1)
    s64 = np.concatenate([-sn, sn], axis=1)
    return c64, s64


def _eoh2():
    one = np.concatenate([np.tile((np.arange(16) == h).astype(np.float32)[:, None], (1, 128))
                          for h in range(16)], axis=1)
    e = np.zeros((80, 16 * 128), np.float32)
    e[0:16] = one
    e[64:80] = one
    return e.astype(ml_dtypes.bfloat16)


def _masks():
    p = np.arange(128)[:, None]
    c = np.arange(128)[None, :]
    blk = np.float32(-1e30)
    diag = np.where(c >= p, np.float32(0), blk)
    left = np.where(c <= p, np.float32(0), blk)
    mstd = np.concatenate([diag, left], axis=1)
    return mstd, left


def kernel(L0_attn_x, L0_attn_Wq, L0_attn_bq, L0_attn_Wk, L0_attn_bk,
           L0_attn_Wv, L0_attn_bv, L0_attn_Wo, L0_attn_bo,
           L0_ffn_ternary_weight, L0_ffn_bias, L0_ffn_act_scale):
    bfdt = ml_dtypes.bfloat16
    x = np.ascontiguousarray(np.asarray(L0_attn_x, dtype=np.float32))
    mstd, left = _masks()
    nc = build_nc()
    in_maps = []
    bcast = lambda v: np.ascontiguousarray(
        np.broadcast_to(np.asarray(v, np.float32)[None, :], (128, D)).astype(bfdt))
    common = {
        "wq": np.ascontiguousarray(np.asarray(L0_attn_Wq, np.float32)),
        "wk": np.ascontiguousarray(np.asarray(L0_attn_Wk, np.float32)),
        "wv": np.ascontiguousarray(np.asarray(L0_attn_Wv, np.float32)),
        "wo": np.ascontiguousarray(np.asarray(L0_attn_Wo, np.float32)),
        "tern": np.ascontiguousarray(np.asarray(L0_ffn_ternary_weight, np.int32)),
        "bqh": bcast(L0_attn_bq), "bkh": bcast(L0_attn_bk),
        "bvh": bcast(L0_attn_bv), "boh": bcast(L0_attn_bo),
        "bfh": bcast(L0_ffn_bias),
        "ash": np.full((128, 1), float(np.asarray(L0_ffn_act_scale)), np.float32),
        "msh": mstd.astype(bfdt),
        "idfh": np.eye(128, dtype=np.float32),
        "idbh": np.eye(128, dtype=np.float32).astype(bfdt),
        "eohh": _eoh2()[0:16],
    }
    tabs = {}
    for h in (0, 1):
        c64, s64 = _tables(h)
        tabs[h] = {"ckh": c64.astype(bfdt), "skh": s64.astype(bfdt)}
    mA1_h0 = np.concatenate([np.full((128, 128), -1e30, np.float32),
                             np.zeros((128, 128), np.float32)], axis=1)
    for core in range(8):
        b, h = core // 2, core % 2
        if h == 0:
            halo = np.zeros((128, D), np.float32)
        else:
            halo = x[b, SC - 128:SC, :]
        xin = np.ascontiguousarray(np.concatenate([halo, x[b, h * SC:(h + 1) * SC, :]], axis=0))
        m = dict(common)
        m["xin"] = xin
        m.update(tabs[h])
        m["mj0h"] = (left if h == 1 else np.full((128, 128), -1e30, np.float32)).astype(bfdt)
        m["mA1h"] = (mA1_h0 if h == 0 else np.zeros((128, 256), np.float32)).astype(bfdt)
        in_maps.append(m)
    res = run_bass_kernel_spmd(nc, in_maps, core_ids=list(range(8)))
    out = np.empty((B, S, D), np.float32)
    for core in range(8):
        b, h = core // 2, core % 2
        out[b, h * SC:(h + 1) * SC, :] = res.results[core]["xout"]
    return out

